# version 13
# baseline (speedup 1.0000x reference)
import numpy as np
import ml_dtypes
from contextlib import ExitStack

import concourse.bass as bass
import concourse.mybir as mybir
from concourse.bass_utils import run_bass_kernel_spmd

F32 = mybir.dt.float32
BF16 = mybir.dt.bfloat16
I16 = mybir.dt.int16
U16 = mybir.dt.uint16
U8 = mybir.dt.uint8
AF = mybir.ActivationFunctionType
ALU = mybir.AluOpType
AX = mybir.AxisListType

D = 1024
T = 4096
TT = 512
NT = T // TT
H = 4
DV = 2048
SC = 128
NSC = T // SC
FF = 3072
LN16 = float(np.log(16.0))
QMAX = 32767.0

DEBUG_TAPS = set()
PHASES = 99
BP = 99


class Bld:
    def __init__(self, taps=()):
        self.nc = bass.Bass("TRN2", target_bir_lowering=False, dynamic_dma_scratch_size=8192)
        self.ctx = ExitStack()
        self.taps = set(taps)
        self.tapn = {}
        self.dctr = 0

    def sb(self, name, shape, dt):
        return self.ctx.enter_context(self.nc.sbuf_tensor(name, shape, dt))

    def ps(self, name, shape, dt=F32):
        return self.ctx.enter_context(self.nc.psum_tensor(name, shape, dt))

    ENGS = ("te", "ve", "se", "ge", "sy")

    def bar(self):
        self.nc.all_engine_barrier()
        for e in self.ENGS:
            for k in self.mc:
                self.lastw[(e, k)] = self.mc[k]

    def eng(self, name):
        return {"te": self.nc.tensor, "ve": self.nc.vector, "se": self.nc.scalar,
                "ge": self.nc.gpsimd, "sy": self.nc.sync}[name]

    def tag(self, key, inst):
        inst.then_inc(self.sems[key], 1)
        self.mc[key] += 1
        return self.mc[key]

    def w(self, ename, key, val):
        if val is None or val <= 0:
            return
        if self.lastw.get((ename, key), 0) >= val:
            return
        self.eng(ename).wait_ge(self.sems[key], val)
        self.lastw[(ename, key)] = val

    def dma(self, out, in_, transpose=False):
        self.nc.sync.dma_start(out=out, in_=in_, transpose=transpose).then_inc(
            self.dsem, 16)
        self.dctr += 16
        self.mc["S"] = self.dctr
        return self.dctr

    def dmag(self, out, in_):
        self.nc.gpsimd.dma_start(out=out, in_=in_).then_inc(self.gsem, 16)
        self.gctr += 16
        self.mc["G"] = self.gctr
        return self.gctr

    def vck(self, inst):
        v = self.tag("V", inst)
        self.nc.vector.wait_ge(self.sems["V"], v)
        self.lastw[("ve", "V")] = v
        return v

    def ack(self, inst):
        v = self.tag("A", inst)
        self.nc.scalar.wait_ge(self.sems["A"], v)
        self.lastw[("se", "A")] = v
        return v

    def dmaq(self, qi, out, in_, transpose=False, queue="sy"):
        eng = self.nc.sync
        k = f"q{qi}"
        eng.dma_start(out=out, in_=in_, transpose=transpose).then_inc(self.sems[k], 16)
        self.mc[k] += 16
        return self.mc[k]

    def fl(self):
        self.nc.sync.wait_ge(self.dsem, self.dctr)
        for i in range(8):
            k = f"q{i}"
            if self.mc[k]:
                self.nc.sync.wait_ge(self.sems[k], self.mc[k])
        if self.gctr:
            self.nc.gpsimd.wait_ge(self.gsem, self.gctr)
        self.bar()

    def tap(self, name, ap):
        if name not in self.taps:
            return
        k = self.tapn.get(name, 0)
        self.tapn[name] = k + 1
        t = self.nc.dram_tensor(f"tap_{name}_{k}", list(ap.shape), ap.dtype,
                                kind="ExternalOutput")
        idx = tuple(slice(0, s) for s in ap.shape)
        self.dma(t[idx], ap)
        self.fl()

    def pingpong(self, n, emit_mms, emit_evac, banks):
        nc = self.nc
        nb = len(banks)
        pe0 = self.ppe_ctr
        ev0 = self.pev_ctr
        for i in range(n):
            if i >= nb:
                nc.tensor.wait_ge(self.sev, ev0 + i - nb + 1)
            emit_mms(i, banks[i % nb]).then_inc(self.spe, 1)
            self.ppe_ctr += 1
        for i in range(n):
            nc.scalar.wait_ge(self.spe, pe0 + i + 1)
            emit_evac(i, banks[i % nb]).then_inc(self.sev, 1)
            self.pev_ctr += 1
        self.bar()

    def build(self):
        nc = self.nc
        di = {}
        for name, shape in [
            ("xT", [D, T]), ("Wq", [D, D]), ("Wk", [D, D]), ("Wv", [D, DV]),
            ("Wg", [D, DV]), ("Wgk1", [D, 16]), ("Wgk2", [16, D]),
            ("bgk2f", [128, 8]), ("rmsw", [1, 512]), ("Wo", [DV, D]),
            ("wup", [FF, D]), ("wgate", [FF, D]), ("wdown", [D, FF]),
            ("wmod", [D, 6 * D]), ("cfm", [128, 8]), ("selS", [1, 8]),
            ("oneh", [1, 8]),
        ]:
            di[name] = nc.dram_tensor(name, shape, F32, kind="ExternalInput")
        outT = nc.dram_tensor("outT", [D, T], F32, kind="ExternalOutput")

        d_h1 = nc.dram_tensor("d_h1", [D, T], BF16)
        d_h2 = nc.dram_tensor("d_h2", [D, T], BF16)
        d_xmid = nc.dram_tensor("d_xmid", [D, T], F32)
        d_oloc = nc.dram_tensor("d_oloc", [NSC, H, 128, 512], BF16)
        d_qc = nc.dram_tensor("d_qc", [NSC, D, 128], BF16)
        d_kh = nc.dram_tensor("d_kh", [2, D, 128], BF16)
        d_sin = nc.dram_tensor("d_sin", [D, 512], BF16)
        d_sout = nc.dram_tensor("d_sout", [8 * D, 512], BF16)
        d_up = nc.dram_tensor("d_up", [24, 128, T], BF16)
        d_gate = nc.dram_tensor("d_gate", [24, 128, T], BF16)
        d_t2 = nc.dram_tensor("d_t2", [24, 128, T], BF16)
        d_opre = nc.dram_tensor("d_opre", [24, 128, T], BF16)
        d_wdq = nc.dram_tensor("d_wdq", [2, FF, D], BF16)
        d_wdqd = nc.dram_tensor("d_wdqd", [D, FF], BF16)
        d_scal = nc.dram_tensor("d_scal", [1, 64], F32)
        d_pmax = nc.dram_tensor("d_pmax", [128, 1], F32)
        ar_io = [(nc.dram_tensor(f"d_arin{k}", [1, 16], F32),
                  nc.dram_tensor(f"d_arout{k}", [1, 16], F32)) for k in range(4)]
        mask_np = np.triu(np.ones((128, 128), np.float32)).astype(ml_dtypes.bfloat16)
        d_mask = nc.inline_tensor(mask_np, "d_mask")

        WAC = 32896
        wa = self.sb("wa", [128, WAC], BF16)
        f_x = self.sb("f_x", [128, 4096], F32)
        f_P = self.sb("f_P", [128, 4096], F32)
        f_S = self.sb("f_S", [128, 4096], F32)
        f_c0 = self.sb("f_c0", [128, 4096], F32)
        f_c1 = self.sb("f_c1", [128, 1280], F32)
        f_m = self.sb("f_m", [128, 96], F32)
        cp = self.sb("cp", [128, 10], F32)
        cm1 = self.sb("cm1", [128, 1], F32)
        b_q = self.sb("b_q", [128, 4096], BF16)
        b_k = self.sb("b_k", [128, 4096], BF16)
        b_v = self.sb("b_v", [128, 8192], BF16)
        b_h1 = self.sb("b_h1", [128, 4096], BF16)
        b_xb = self.sb("b_xb", [128, 4096], BF16)
        b_e = self.sb("b_e", [128, 2048], BF16)
        b_sbf = self.sb("b_sbf", [128, 4096], BF16)
        b_st = self.sb("b_st", [128, 4096], BF16)
        b_kh = self.sb("b_kh", [128, 1024], BF16)
        b_ms = self.sb("b_ms", [128, 128], BF16)
        b_rw = self.sb("b_rw", [1, 512], BF16)
        b_on = self.sb("b_on", [128, 1], BF16)
        b_or = self.sb("b_or", [1, 128], BF16)
        wgk2 = self.sb("wgk2", [16, 1024], BF16)
        bgk2 = self.sb("bgk2", [128, 16], F32)
        u8m = self.sb("u8m", [128, 8], U8)
        i_carve = f_S.bitcast(I16)
        u_carve = f_S.bitcast(U16)
        PS = [self.ps(f"PS{i}", [128, 512]) for i in range(8)]

        self.dsem = self.ctx.enter_context(nc.semaphore("dS"))
        self.gsem = self.ctx.enter_context(nc.semaphore("dG"))
        self.spe = self.ctx.enter_context(nc.semaphore("sPE"))
        self.sev = self.ctx.enter_context(nc.semaphore("sEV"))
        ccsem = self.ctx.enter_context(nc.semaphore("sCC"))
        self.sems = {k: self.ctx.enter_context(nc.semaphore("m" + k))
                     for k in ("T", "V", "A", "P")}
        self.sems["S"] = self.dsem
        self.sems["G"] = self.gsem
        for i in range(8):
            self.sems[f"q{i}"] = self.ctx.enter_context(nc.semaphore(f"q{i}"))
        self.mc = {"T": 0, "V": 0, "A": 0, "P": 0, "S": 0, "G": 0}
        for i in range(8):
            self.mc[f"q{i}"] = 0
        self.lastw = {}
        self.gctr = 0
        self.ppe_ctr = 0
        self.pev_ctr = 0
        cc = [0]

        te, ve, se, ge = nc.tensor, nc.vector, nc.scalar, nc.gpsimd

        CONSTS = [-LN16, -1.0 / 16.0, 1.0 / 16.0, -0.5, 1e-5, 1.0 / 1024.0,
                  1.0 / 512.0, 1.0, 65536.0]
        for j, v in enumerate(CONSTS):
            ge.memset(cp[:, j:j + 1], float(v))
        ge.memset(cm1[:, :], -1.0)

        def cv(v):
            return cp[:, CONSTS.index(v):CONSTS.index(v) + 1]

        def cv1(v):
            return cp[0:1, CONSTS.index(v):CONSTS.index(v) + 1]

        M_SA, M_CA1, M_GA, M_SB, M_CB1, M_GB = 0, 8, 16, 24, 32, 40
        M_EXPG, M_EXPL, M_LAM = 48, 56, 64
        M_RM0, M_RM1 = 72, 73
        M_SCAL = 74

        ge.memset(f_m[:, M_LAM:M_LAM + 8], 0.0)
        ge.memset(f_m[:, M_RM0:M_RM0 + 2], 0.0)
        ge.memset(b_on[:, :], 1.0)
        ge.memset(b_or[:, :], 1.0)
        ones_col = b_on[:, 0:1]
        ones_row = b_or[0:1, :]

        self.dma(b_ms[:, :], d_mask[:, :])
        self.dma(f_m[:, M_SA:M_SA + 8], di["cfm"][:, :])
        self.dma(f_c1[0:1, 0:512], di["rmsw"][:, :])
        self.fl()
        ve.tensor_copy(b_rw[:, :], f_c1[0:1, 0:512])
        se.activation(f_m[:, M_SA:M_SA + 8], f_m[:, M_SA:M_SA + 8], AF.Silu)
        self.bar()

        siluc = f_m[:, M_SA:M_SA + 8]
        ge.memset(f_c0[:, 0:48], 0.0)
        self.bar()
        for ic in range(8):
            for hh in range(2):
                self.dma(f_x[:, 0:3072], di["wmod"][128 * ic:128 * (ic + 1),
                                                    3072 * hh:3072 * (hh + 1)])
                self.fl()
                for j24 in range(24):
                    jc = 24 * hh + j24
                    te.matmul(PS[0][:, jc:jc + 1], f_x[:, 128 * j24:128 * (j24 + 1)],
                              siluc[:, ic:ic + 1], start=True, stop=True)
                self.bar()
            se.copy(f_c1[:, 16:64], PS[0][:, 0:48])
            self.bar()
            ve.tensor_tensor(f_c0[:, 0:48], f_c0[:, 0:48], f_c1[:, 16:64], ALU.add)
            self.bar()
        ve.tensor_copy(f_m[:, 0:48], f_c0[:, 0:48])
        self.bar()
        ve.tensor_scalar(f_m[:, M_CA1:M_CA1 + 8], f_m[:, M_CA1:M_CA1 + 8], 1.0, None,
                         ALU.add)
        ve.tensor_scalar(f_m[:, M_CB1:M_CB1 + 8], f_m[:, M_CB1:M_CB1 + 8], 1.0, None,
                         ALU.add)
        self.bar()
        for off in (M_GA, M_GB):
            g = f_m[:, off:off + 8]
            ve.tensor_scalar(u8m[:, :], g, 0.0, None, ALU.is_ge)
            ve.tensor_scalar(f_c1[:, 0:8], g, 0.25, None, ALU.max)
            ve.tensor_scalar(f_c1[:, 8:16], g, -0.25, None, ALU.min)
            self.bar()
            ve.select(g, u8m[:, :], f_c1[:, 0:8], f_c1[:, 8:16])
            self.bar()
        self.tap("mod", f_m[:, 0:48])
        if PHASES < 1:
            return self.finish()

        WQ, WK, WV, GK1 = 0, 8192, 16384, 32768

        def load_w(dst_off, src, ic_cols, n_ic):
            for ic in range(n_ic):
                self.dma(f_x[:, 0:ic_cols], src[128 * ic:128 * (ic + 1), :])
                self.fl()
                ve.tensor_copy(
                    wa[:, dst_off + ic * ic_cols:dst_off + (ic + 1) * ic_cols],
                    f_x[:, 0:ic_cols])
                self.bar()

        load_w(WQ, di["Wq"], 1024, 8)
        load_w(WK, di["Wk"], 1024, 8)
        load_w(WV, di["Wv"], 2048, 8)
        load_w(GK1, di["Wgk1"], 16, 8)
        self.dma(f_c1[0:16, 0:1024], di["Wgk2"][:, :])
        self.fl()
        ve.tensor_copy(wgk2[:, :], f_c1[0:16, 0:1024])
        self.bar()
        self.dma(f_c1[:, 0:8], di["bgk2f"][:, :])
        self.fl()
        ve.tensor_copy(bgk2[:, 0:8], f_c1[:, 0:8])
        ve.tensor_scalar(bgk2[:, 8:16], f_c1[:, 0:8], -1.0, None, ALU.mult)
        self.bar()
        ge.memset(f_S[:, :], 0.0)
        ge.memset(b_sbf[:, :], 0.0)
        self.bar()

        def ln_block(src_b, sq_b, sa_off, ca1_off, out_b):
            se.activation(sq_b[:, :], src_b[:, :], AF.Square)
            self.bar()
            for ic in range(8):
                te.matmul(PS[2][0:1, :], ones_col, src_b[:, 512 * ic:512 * (ic + 1)],
                          start=(ic == 0), stop=(ic == 7))
            for ic in range(8):
                te.matmul(PS[3][0:1, :], ones_col, sq_b[:, 512 * ic:512 * (ic + 1)],
                          start=(ic == 0), stop=(ic == 7))
            self.bar()
            st = f_c1[0:1, :]
            ve.tensor_scalar(st[:, 0:512], PS[2][0:1, :], 1.0 / 1024.0, None, ALU.mult)
            ve.tensor_scalar(st[:, 512:1024], PS[3][0:1, :], 1.0 / 1024.0, None,
                             ALU.mult)
            self.bar()
            ve.tensor_tensor(f_c0[0:1, 0:512], st[:, 0:512], st[:, 0:512], ALU.mult)
            self.bar()
            ve.tensor_tensor(st[:, 512:1024], st[:, 512:1024], f_c0[0:1, 0:512],
                             ALU.subtract)
            self.bar()
            se.activation(st[:, 512:1024], st[:, 512:1024], AF.Ln, bias=cv1(1e-5))
            self.bar()
            se.activation(st[:, 512:1024], st[:, 512:1024], AF.Exp, scale=cv1(-0.5))
            self.bar()
            ve.tensor_tensor(st[:, 0:512], st[:, 0:512], st[:, 512:1024], ALU.mult)
            self.bar()
            rstd_b = b_e[0:1, 0:512]
            mrstd_b = b_e[0:1, 512:1024]
            ve.tensor_copy(rstd_b, st[:, 512:1024])
            ve.tensor_copy(mrstd_b, st[:, 0:512])
            self.bar()
            te.matmul(PS[4][:, :], ones_row, rstd_b, start=True, stop=True)
            te.matmul(PS[5][:, :], ones_row, mrstd_b, start=True, stop=True)
            self.bar()
            for ic in range(8):
                o = (ic % 4) * 1024
                dst = (f_c0 if ic < 4 else f_P)
                ve.tensor_tensor(dst[:, o:o + 512], src_b[:, 512 * ic:512 * (ic + 1)],
                                 PS[4][:, :], ALU.mult)
            self.bar()
            for ic in range(8):
                o = (ic % 4) * 1024
                dst = (f_c0 if ic < 4 else f_P)
                ve.tensor_tensor(dst[:, o:o + 512], dst[:, o:o + 512], PS[5][:, :],
                                 ALU.subtract)
            self.bar()
            for ic in range(8):
                o = (ic % 4) * 1024
                dst = (f_c0 if ic < 4 else f_P)
                se.activation(out_b[:, 512 * ic:512 * (ic + 1)], dst[:, o:o + 512],
                              AF.Identity, bias=f_m[:, sa_off + ic:sa_off + ic + 1],
                              scale=f_m[:, ca1_off + ic:ca1_off + ic + 1])
            self.bar()

        for tt in range(NT):
            tcols = slice(TT * tt, TT * (tt + 1))
            self.dma(f_x[:, :].rearrange("p (ic c) -> p ic c", c=TT),
                     di["xT"][:, tcols].rearrange("(ic p) c -> p ic c", p=128))
            self.fl()
            ve.tensor_copy(b_xb[:, :], f_x[:, :])
            self.bar()
            ln_block(b_xb, b_h1, M_SA, M_CA1, b_h1)
            self.tap("h1", b_h1[:, 0:1024])
            self.dma(d_h1[:, tcols].rearrange("(ic p) c -> p ic c", p=128),
                     b_h1[:, :].rearrange("p (ic c) -> p ic c", c=TT))
            if PHASES < 2:
                self.fl()
                continue

            def qk(w_off, out_sb):
                def mms(i, bank):
                    for ic in range(8):
                        last = te.matmul(
                            bank[:, :],
                            wa[:, w_off + ic * 1024 + 128 * i:
                               w_off + ic * 1024 + 128 * (i + 1)],
                            b_h1[:, 512 * ic:512 * (ic + 1)],
                            start=(ic == 0), stop=(ic == 7))
                    return last

                def evac(i, bank):
                    return se.copy(out_sb[:, 512 * i:512 * (i + 1)], bank[:, :])
                self.pingpong(8, mms, evac, [PS[0], PS[1]])
            qk(WQ, b_q)
            qk(WK, b_k)
            self.tap("q", b_q[:, 0:1024])
            self.tap("k", b_k[:, 0:1024])

            def v_mms(i, bank):
                tc, eg = divmod(i, 4)
                for ic in range(8):
                    last = te.matmul(
                        bank[:, :],
                        b_h1[:, 512 * ic + 128 * tc:512 * ic + 128 * (tc + 1)],
                        wa[:, WV + ic * 2048 + 512 * eg:WV + ic * 2048 + 512 * (eg + 1)],
                        start=(ic == 0), stop=(ic == 7))
                return last

            def v_evac(i, bank):
                tc, eg = divmod(i, 4)
                return se.copy(
                    b_v[:, tc * 2048 + 512 * eg:tc * 2048 + 512 * (eg + 1)], bank[:, :])
            self.pingpong(16, v_mms, v_evac, [PS[0], PS[1]])
            self.tap("v", b_v[:, 0:2048])

            for ic in range(8):
                te.matmul(PS[0][0:16, :], wa[:, GK1 + 16 * ic:GK1 + 16 * (ic + 1)],
                          b_h1[:, 512 * ic:512 * (ic + 1)], start=(ic == 0),
                          stop=(ic == 7))
            self.bar()
            t1b = b_e[0:16, 1024:1536]
            se.copy(t1b, PS[0][0:16, :])
            self.bar()
            for r in range(4):
                for half in range(2):
                    ic = 2 * r + half
                    te.matmul(PS[3 + half][:, :], wgk2[:, 128 * ic:128 * (ic + 1)],
                              t1b, start=True, stop=True)
                self.bar()
                u0 = f_c0[:, 0:1024]
                aa = f_c0[:, 1024:2048]
                bb = f_c0[:, 2048:3072]
                for half in range(2):
                    ic = 2 * r + half
                    se.activation(u0[:, 512 * half:512 * (half + 1)],
                                  PS[3 + half][:, :], AF.Identity,
                                  bias=bgk2[:, ic:ic + 1])
                self.bar()
                se.activation(aa, u0, AF.Abs)
                self.bar()
                se.activation(bb, aa, AF.Exp, scale=cm1[:, 0:1])
                self.bar()
                se.activation(aa, bb, AF.Ln, bias=cv(1.0))
                self.bar()
                ve.tensor_scalar(bb, u0, -1.0, 0.0, ALU.mult, ALU.max)
                self.bar()
                ve.tensor_tensor(f_P[:, 1024 * r:1024 * (r + 1)], aa, bb, ALU.add)
                self.bar()
            self.tap("P", f_P[:, 0:1024])
            if PHASES < 3:
                continue

            scst = getattr(self, "_scst", None)
            if scst is None:
                scst = self._scst = {"pm": 0, "ee": 0, "kq": 0, "mk": 0, "oh": 0,
                                     "evA": 0, "sb1": 0, "stt": 0, "lam": 0,
                                     "kw": {}, "kr": {}, "qout": {}, "osp": {}}
            for s4 in range(4):
                sc = 4 * tt + s4
                co = 128 * s4
                pq = sc % 2
                self.w("ve", "A", scst["ee"])
                for ic in range(8):
                    base = ic * 512 + co
                    last = ve.tensor_scalar(f_P[:, base + 64:base + 128],
                                            f_P[:, base + 64:base + 128],
                                            f_P[:, base + 63:base + 64], None,
                                            ALU.add)
                scst["pm"] = self.tag("V", last)
                pv = f_P[:, :].rearrange("p (ic c) -> p ic c", c=512)[:, :, co:co + 128]
                pg = f_P[:, :].rearrange("p (ic c) -> p ic c",
                                         c=512)[:, :, co + 127:co + 128]
                ep = b_e[:, 0:1024].rearrange("p (ic c) -> p ic c", c=128)
                em = b_e[:, 1024:2048].rearrange("p (ic c) -> p ic c", c=128)
                self.w("se", "V", scst["pm"])
                self.w("se", "V", scst["lam"])
                se.activation(ep, pv, AF.Exp, scale=cv(-1.0 / 16.0), bias=cv(-LN16))
                se.activation(em, pv, AF.Exp, scale=cv(1.0 / 16.0))
                se.activation(f_m[:, M_EXPG:M_EXPG + 8].rearrange(
                    "p (a b) -> p a b", b=1), pg, AF.Exp, scale=cv(-1.0 / 16.0))
                scst["ee"] = self.tag("A", se.activation(
                    f_m[:, M_EXPL:M_EXPL + 8], f_m[:, M_LAM:M_LAM + 8],
                    AF.Exp, scale=cv(-1.0 / 16.0)))
                qv = b_q[:, :].rearrange("p (dk c) -> p dk c", c=512)[:, :, co:co + 128]
                kv = b_k[:, :].rearrange("p (dk c) -> p dk c", c=512)[:, :, co:co + 128]
                self.w("ve", "A", scst["ee"])
                ve.tensor_tensor(qv, qv, ep, ALU.mult)
                self.vck(ve.tensor_tensor(kv, kv, em, ALU.mult))
                self.w("ve", f"q{4 + (1 - pq)}", scst["qout"].get(sc - 1, 0))
                self.w("ve", f"q{4 + pq}", scst["kr"].get(sc - 2, 0))
                ve.tensor_tensor(
                    b_st[:, 0:1024].rearrange("p (dc c) -> p dc c", c=128), kv,
                    bass.AP(f_m, M_EXPG, [[96, 128], [1, 8], [0, 128]]), ALU.mult)
                scst["kq"] = self.tag("V", ve.tensor_tensor(
                    b_st[:, 1024:2048].rearrange("p (dc c) -> p dc c", c=128), qv,
                    bass.AP(f_m, M_EXPL, [[96, 128], [1, 8], [0, 128]]), ALU.mult))
                self.w("sy", "V", scst["kq"])
                kw = self.dmaq(4 + pq, d_kh[pq].rearrange("(dc p) c -> p dc c", p=128),
                               b_st[:, 0:1024].rearrange("p (dc c) -> p dc c", c=128))
                scst["kw"][sc] = kw
                scst["qout"][sc] = self.dmaq(
                    4 + pq, d_qc[sc].rearrange("(dc p) c -> p dc c", p=128),
                    b_st[:, 1024:2048].rearrange("p (dc c) -> p dc c", c=128))
                self.w("sy", f"q{4 + pq}", scst["qout"][sc])
                self.w("sy", "T", scst.get("ut", 0))
                scst["kr"][sc] = self.dmaq(4 + pq, b_kh[:, :], d_kh[pq][:, :],
                                           transpose=True)
                self.w("te", "V", scst["kq"])
                for h in range(H):
                    for dc in range(2):
                        base = (2 * h + dc) * 512 + co
                        last = te.matmul(PS[2][:, 128 * h:128 * (h + 1)],
                                         b_k[:, base:base + 128],
                                         b_q[:, base:base + 128],
                                         start=(dc == 0), stop=(dc == 1))
                at = self.tag("T", last)
                self.w("ve", "T", at)
                scst["mk"] = self.tag("V", ve.tensor_tensor(
                    b_st[:, 2048:2560].rearrange("p (h c) -> p h c", c=128),
                    PS[2][:, :].rearrange("p (h c) -> p h c", c=128),
                    bass.AP(b_ms, 0, [[128, 128], [0, 4], [1, 128]]), ALU.mult))
                ohh = {}
                self.w("te", "V", scst["mk"])
                self.w("te", "A", scst["sb1"])
                for h in range(H):
                    pso = PS[3 + h]
                    for m4 in range(4):
                        oc = 128 * m4
                        for dc in range(2):
                            blk = 2 * h + dc
                            te.matmul(pso[:, oc:oc + 128],
                                      b_sbf[:, 512 * blk + oc:512 * blk + oc + 128],
                                      b_q[:, blk * 512 + co:blk * 512 + co + 128],
                                      start=(dc == 0), stop=False)
                        last = te.matmul(pso[:, oc:oc + 128],
                                         b_v[:, s4 * 2048 + 512 * h + oc:
                                            s4 * 2048 + 512 * h + oc + 128],
                                         b_st[:, 2048 + 128 * h:2048 + 128 * (h + 1)],
                                         start=False, stop=True)
                    ohh[h] = self.tag("T", last)
                scst["oh"] = ohh[3]
                for h in range(H):
                    stg = b_st[:, 2560 + 512 * (h % 2):3072 + 512 * (h % 2)]
                    self.w("se", "T", ohh[h])
                    self.w("se", f"q{6 + h % 2}", scst["osp"].get((sc, h - 2), 0)
                           if h >= 2 else scst["osp"].get((sc - 1, 2 + h), 0))
                    ea = self.tag("A", se.copy(stg, PS[3 + h][:, :]))
                    self.w("sy", "A", ea)
                    scst["osp"][(sc, h)] = self.dmaq(6 + h % 2, d_oloc[sc, h], stg)
                scst["evA"] = ea
                for rnd in range(2):
                    banks = [PS[0], PS[1], PS[7], PS[2]]
                    self.w("te", f"q{4 + pq}", scst["kr"][sc])
                    if rnd == 1:
                        self.w("te", "V", scst["stt"])
                    for j in range(4):
                        blk = 4 * rnd + j
                        h = blk // 2
                        last = te.matmul(banks[j][:, :],
                                         b_kh[:, 128 * blk:128 * (blk + 1)],
                                         b_v[:, s4 * 2048 + 512 * h:
                                            s4 * 2048 + 512 * (h + 1)],
                                         start=True, stop=True)
                    ut = self.tag("T", last)
                    scst["ut"] = ut
                    self.w("ve", "T", ut)
                    for j in range(4):
                        blk = 4 * rnd + j
                        last = ve.scalar_tensor_tensor(
                            f_S[:, 512 * blk:512 * (blk + 1)],
                            f_S[:, 512 * blk:512 * (blk + 1)],
                            f_m[:, M_EXPG + blk:M_EXPG + blk + 1],
                            banks[j][:, :], ALU.mult, ALU.add)
                    scst["stt"] = self.tag("V", last)
                    self.w("se", "V", scst["stt"])
                    for j in range(4):
                        blk = 4 * rnd + j
                        last = se.copy(b_sbf[:, 512 * blk:512 * (blk + 1)],
                                       f_S[:, 512 * blk:512 * (blk + 1)])
                    scst["sb1"] = self.tag("A", last)
                scst["lam"] = self.tag("V", ve.tensor_tensor(
                    f_m[:, M_LAM:M_LAM + 8], f_m[:, M_LAM:M_LAM + 8],
                    pg.rearrange("p a b -> p (a b)"), ALU.add))
            self.bar()
            self.tap("S", f_S[:, 0:1024])
        if PHASES < 4:
            return self.finish()

        self.dma(d_sin[:, :].rearrange("(blk p) c -> p blk c", p=128),
                 b_sbf[:, :].rearrange("p (blk c) -> p blk c", c=512))
        self.fl()
        ge.collective_compute("AllGather", ALU.bypass,
                              replica_groups=[list(range(8))],
                              ins=[d_sin[:, :]], outs=[d_sout[:, :]]).then_inc(ccsem)
        cc[0] += 1
        ge.wait_ge(ccsem, cc[0])
        self.bar()
        self.dma(f_c1[:, 0:8], bass.AP(di["selS"], 0, [[0, 128], [1, 8]]))
        self.fl()
        ge.memset(f_S[:, :], 0.0)
        self.bar()
        for r in range(8):
            self.dma(b_st[:, :].rearrange("p (blk c) -> p blk c", c=512),
                     d_sout[D * r:D * (r + 1), :].rearrange("(blk p) c -> p blk c",
                                                            p=128))
            self.fl()
            ve.scalar_tensor_tensor(f_S[:, :], b_st[:, :], f_c1[:, r:r + 1], f_S[:, :],
                                    ALU.mult, ALU.add)
            self.bar()
        ve.tensor_copy(b_sbf[:, :], f_S[:, :])
        self.bar()
        self.tap("Sin", b_sbf[:, 0:1024])

        WO, WGB = 0, 16384
        load_w(WO, di["Wo"], 1024, 16)
        load_w(WGB, di["Wg"], 2048, 8)

        for tt in range(NT):
            tcols = slice(TT * tt, TT * (tt + 1))
            self.dmaq(2, f_x[:, :].rearrange("p (ic c) -> p ic c", c=TT),
                      di["xT"][:, tcols].rearrange("(ic p) c -> p ic c", p=128))
            lxh = getattr(self, "_lxh", {})
            self._lxh = lxh
            lxh[tt] = self.dmaq(2, b_h1[:, :].rearrange("p (ic c) -> p ic c", c=TT),
                                d_h1[:, tcols].rearrange("(ic p) c -> p ic c", p=128))
            self.fl()

            def gt_mms(i, bank):
                for ic in range(8):
                    last = te.matmul(
                        bank[:, :],
                        wa[:, WGB + ic * 2048 + 128 * i:WGB + ic * 2048 + 128 * (i + 1)],
                        b_h1[:, 512 * ic:512 * (ic + 1)],
                        start=(ic == 0), stop=(ic == 7))
                return last

            def gt_evac(i, bank):
                return se.activation(b_v[:, 512 * i:512 * (i + 1)], bank[:, :], AF.Silu)
            self.pingpong(16, gt_mms, gt_evac, [PS[0], PS[1]])
            self.tap("sgt", b_v[:, 0:1024])

            corrbank = [PS[3], PS[5], PS[2], PS[6]]
            c1 = {}
            v1 = {}
            a1 = {}
            c2 = {}
            a2 = {}
            c3 = {}
            v2 = {}
            vstt = {}
            lsc = {}
            ofull = [f_c1[:, 0:512], f_c1[:, 512:1024],
                     f_c0[:, 0:512], f_c0[:, 512:1024]]
            qcb = [b_st[:, 0:1024], b_h1[:, 0:1024]]

            def sc_loads(s4):
                if s4 > 3:
                    return
                sc = 4 * tt + s4
                pp = s4 % 2
                for h in range(H):
                    self.dmaq(pp, b_k[:, 2048 * pp + 512 * h:2048 * pp + 512 * (h + 1)],
                              d_oloc[sc, h])
                lsc[s4] = self.dmaq(
                    pp, qcb[pp][:, :].rearrange("p (dc c) -> p dc c", c=128),
                    d_qc[sc].rearrange("(dc p) c -> p dc c", p=128))
            sc_loads(0)
            sc_loads(1)
            for s4 in range(4):
                sc = 4 * tt + s4
                pp = s4 % 2
                for h in range(H):
                    self.w("te", f"q{pp}", lsc[s4])
                    self.w("te", "A", a2.get((s4 - 1, h), 0))
                    self.w("te", "V", v1.get((s4 - 1, h), 0))
                    for m4 in range(4):
                        oc = 128 * m4
                        for dc in range(2):
                            blk = 2 * h + dc
                            last = te.matmul(
                                corrbank[h][:, oc:oc + 128],
                                b_sbf[:, 512 * blk + oc:512 * blk + oc + 128],
                                qcb[pp][:, 128 * blk:128 * (blk + 1)],
                                start=(dc == 0), stop=(dc == 1))
                    c1[(s4, h)] = self.tag("T", last)
                if BP >= 1:
                    for h in range(H):
                        self.w("ve", "T", c1[(s4, h)])
                        self.w("ve", f"q{pp}", lsc[s4])
                        v1[(s4, h)] = self.tag("V", ve.tensor_tensor(
                            ofull[h], corrbank[h][:, :],
                            b_k[:, 2048 * pp + 512 * h:2048 * pp + 512 * (h + 1)],
                            ALU.add))
                self.w("sy", "T", c1[(s4, 3)])
                if BP >= 1:
                    self.w("sy", "V", v1[(s4, 3)])
                sc_loads(s4 + 2)
                if BP < 2:
                    continue
                for h in range(H):
                    self.w("se", "V", v1[(s4, h)])
                    a1[(s4, h)] = self.tag("A", se.activation(
                        b_xb[:, 512 * h:512 * (h + 1)], ofull[h], AF.Square))
                for h in range(H):
                    self.w("te", "A", a1[(s4, h)])
                    for m4 in range(4):
                        last = te.matmul(corrbank[h][0:1, 0:128], ones_col,
                                         b_xb[:, 512 * h + 128 * m4:
                                              512 * h + 128 * (m4 + 1)],
                                         start=(m4 == 0), stop=(m4 == 3))
                    c2[(s4, h)] = self.tag("T", last)
                for h in range(H):
                    row = f_c1[0:1, 1024 + 128 * (h % 2):1152 + 128 * (h % 2)]
                    self.w("se", "T", c2[(s4, h)])
                    self.ack(se.activation(row, corrbank[h][0:1, 0:128], AF.Ln,
                                           scale=cv1(1.0 / 512.0), bias=cv1(1e-5)))
                    a2[(s4, h)] = self.tag("A", se.activation(
                        b_e[0:1, 128 * h:128 * (h + 1)], row, AF.Exp,
                        scale=cv1(-0.5)))
                if BP < 3:
                    continue
                for h in range(H):
                    rrb = PS[1] if h % 2 == 0 else PS[4]
                    self.w("te", "A", a2[(s4, h)])
                    self.w("te", "V", v2.get((s4, h - 2), 0) if h >= 2 else
                           v2.get((s4 - 1, 2 + h), 0))
                    for m4 in range(4):
                        last = te.matmul(rrb[:, 128 * m4:128 * (m4 + 1)],
                                         b_rw[:, 128 * m4:128 * (m4 + 1)],
                                         b_e[0:1, 128 * h:128 * (h + 1)],
                                         start=True, stop=True)
                    c3[(s4, h)] = self.tag("T", last)
                    self.w("ve", "T", c3[(s4, h)])
                    self.w("ve", "V", v1[(s4, h)])
                    self.vck(ve.tensor_tensor(ofull[h], ofull[h], rrb[:, :],
                                              ALU.mult))
                    v2[(s4, h)] = self.tag("V", ve.tensor_tensor(
                        b_st[:, 2048 + 512 * h:2048 + 512 * (h + 1)].rearrange(
                            "p (a b) -> p a b", b=128),
                        ofull[h].rearrange("p (a b) -> p a b", b=128),
                        bass.AP(b_v, (4 * h) * 512 + 128 * s4,
                                [[8192, 128], [512, 4], [1, 128]]), ALU.mult))
                if BP < 4:
                    continue
                btag = {}
                for jc in range(8):
                    psb = PS[7] if jc < 4 else PS[0]
                    oc = 128 * (jc % 4)
                    self.w("te", "V", v2[(s4, 3)])
                    self.w("te", "V", vstt.get((s4 - 1, 3 if jc < 4 else 7), 0))
                    first = True
                    for h in range(H):
                        for m4 in range(4):
                            ec = 4 * h + m4
                            last = te.matmul(
                                psb[:, oc:oc + 128],
                                wa[:, WO + ec * 1024 + 128 * jc:
                                   WO + ec * 1024 + 128 * (jc + 1)],
                                b_st[:, 2048 + 512 * h + 128 * m4:
                                     2048 + 512 * h + 128 * (m4 + 1)],
                                start=first, stop=(ec == 15))
                            first = False
                    if jc in (3, 7):
                        btag[jc] = self.tag("T", last)
                for jc in range(8):
                    psb = PS[7] if jc < 4 else PS[0]
                    oc = 128 * (jc % 4)
                    self.w("ve", "T", btag[3 if jc < 4 else 7])
                    self.w("ve", "q2", lxh[tt])
                    vstt[(s4, jc)] = self.tag("V", ve.scalar_tensor_tensor(
                        f_P[:, jc * 512 + 128 * s4:jc * 512 + 128 * (s4 + 1)],
                        psb[:, oc:oc + 128], f_m[:, M_GA + jc:M_GA + jc + 1],
                        f_x[:, jc * 512 + 128 * s4:jc * 512 + 128 * (s4 + 1)],
                        ALU.mult, ALU.add))
            self.bar()
            self.dma(d_xmid[:, tcols].rearrange("(ic p) c -> p ic c", p=128),
                     f_P[:, :].rearrange("p (ic c) -> p ic c", c=TT))
            self.tap("xmid", f_P[:, 0:1024])
            ve.tensor_copy(b_xb[:, :], f_P[:, :])
            self.bar()
            ln_block(b_xb, b_h1, M_SB, M_CB1, b_h1)
            self.tap("h2", b_h1[:, 0:1024])
            for ic in range(8):
                ve.tensor_reduce(f_c1[:, 1024 + ic:1024 + ic + 1],
                                 b_h1[:, 512 * ic:512 * (ic + 1)], AX.X, ALU.max,
                                 apply_absolute_value=True)
            self.bar()
            ve.tensor_reduce(f_c1[:, 1032:1033], f_c1[:, 1024:1032], AX.X, ALU.max)
            self.bar()
            ve.tensor_tensor(f_m[:, M_RM0:M_RM0 + 1], f_m[:, M_RM0:M_RM0 + 1],
                             f_c1[:, 1032:1033], ALU.max)
            self.bar()
            self.dma(d_h2[:, tcols].rearrange("(ic p) c -> p ic c", p=128),
                     b_h1[:, :].rearrange("p (ic c) -> p ic c", c=TT))
            self.fl()
        if PHASES < 5:
            return self.finish()

        def fold_max(cols, ar_k):
            arin, arout = ar_io[ar_k]
            for idx, c in enumerate(cols):
                self.dma(d_pmax[:, :], f_m[:, c:c + 1])
                self.fl()
                self.dma(f_c1[0:1, 0:128], bass.AP(d_pmax, 0, [[1, 1], [1, 128]]))
                self.fl()
                ve.tensor_reduce(f_c1[0:1, 128 + idx:129 + idx], f_c1[0:1, 0:128],
                                 AX.X, ALU.max)
                self.bar()
            self.dma(f_c1[0:1, 256:264], di["oneh"][:, :])
            self.fl()
            for idx in range(len(cols)):
                ve.tensor_scalar(f_c1[0:1, 192 + 8 * idx:200 + 8 * idx],
                                 f_c1[0:1, 256:264], f_c1[0:1, 128 + idx:129 + idx],
                                 None, ALU.mult)
            if len(cols) == 1:
                ge.memset(f_c1[0:1, 200:208], 0.0)
            self.bar()
            self.dma(arin[:, :], f_c1[0:1, 192:208])
            self.fl()
            ge.collective_compute("AllReduce", ALU.add,
                                  replica_groups=[list(range(8))],
                                  ins=[arin[:, :]], outs=[arout[:, :]]).then_inc(ccsem)
            cc[0] += 1
            ge.wait_ge(ccsem, cc[0])
            self.bar()
            self.dma(f_c1[0:1, 0:16], arout[:, :])
            self.fl()
            for idx in range(len(cols)):
                ve.tensor_reduce(f_c1[0:1, 16 + idx:17 + idx],
                                 f_c1[0:1, 8 * idx:8 * idx + 8], AX.X, ALU.max)
            self.bar()
            return [f_c1[0:1, 16 + idx:17 + idx] for idx in range(len(cols))]

        def qscale(maxap, scol, slot):
            ve.tensor_scalar(f_c1[0:1, 32:33], maxap, 1.0 / QMAX, 1e-12, ALU.mult,
                             ALU.max)
            self.bar()
            ve.reciprocal(f_c1[0:1, 33:34], f_c1[0:1, 32:33])
            self.bar()
            self.dma(d_scal[0:1, slot:slot + 2], f_c1[0:1, 32:34])
            self.fl()
            self.dma(f_m[:, scol:scol + 1],
                     bass.AP(d_scal, slot, [[0, 128], [1, 1]]))
            self.dma(f_m[:, scol + 1:scol + 2],
                     bass.AP(d_scal, slot + 1, [[0, 128], [1, 1]]))
            self.fl()

        def bcast_scalar(srcap, dstcol, slot):
            self.dma(d_scal[0:1, slot:slot + 1], srcap)
            self.fl()
            self.dma(f_m[:, dstcol:dstcol + 1],
                     bass.AP(d_scal, slot, [[0, 128], [1, 1]]))
            self.fl()

        mx = fold_max([M_RM0], 0)[0]
        qscale(mx, M_SCAL, 0)
        S0, R0 = M_SCAL, M_SCAL + 1
        self.tap("sc0", f_m[:, 72:80])

        fxi = f_x.bitcast(I16)
        fxu = f_x.bitcast(U16)
        bsti = b_st.bitcast(I16)
        self.bar()

        def dequant(src, dst, n_oc, icols):
            nb = icols // 32
            tiles = [f_c0[:, 0:icols], (f_c1 if icols == 1024 else f_P)[:, 0:icols]]
            ls = {}
            vtile = {}
            vout = {}
            gt_ = {}
            ls[0] = self.dmaq(0, tiles[0], src[0:128, :])
            if n_oc > 1:
                ls[1] = self.dmaq(1, tiles[1], src[128:256, :])
            for oc in range(n_oc):
                p = oc % 2
                tile = tiles[p]
                so = (1024 + p * 128) if icols == 1024 else (p * 192)
                sv = f_c1[:, so:so + nb]
                rvo = so + (64 if icols == 1024 else 96)
                rv = f_c1[:, rvo:rvo + nb]
                self.w("ve", f"q{p}", ls[oc])
                self.vck(ve.tensor_reduce(sv.rearrange("p (a b) -> p a b", b=1),
                                 tile.rearrange("p (a b) -> p a b", b=32), AX.X,
                                 ALU.max, apply_absolute_value=True))
                self.vck(ve.tensor_scalar(sv, sv, 1.0 / 7.0, 1e-12, ALU.mult,
                                          ALU.max))
                self.vck(ve.reciprocal(rv, sv))
                ki = i_carve[:, 0:icols] if p == 0 else fxi[:, 0:icols]
                vtile[oc] = self.vck(ve.tensor_tensor(
                    ki.rearrange("p (a b) -> p a b", b=32),
                    tile.rearrange("p (a b) -> p a b", b=32),
                    bass.AP(f_c1, rvo, [[1280, 128], [1, nb], [0, 32]]), ALU.mult))
                if oc + 2 < n_oc:
                    self.w("sy", "V", vtile[oc])
                    ls[oc + 2] = self.dmaq(p, tiles[p],
                                           src[128 * (oc + 2):128 * (oc + 3), :])
                stage = b_st[:, 0:icols] if p == 0 else b_xb[:, 0:icols]
                self.w("ve", f"q{2 + p}", gt_.get(oc - 2, 0))
                vout[oc] = self.tag("V", ve.tensor_tensor(
                    stage.rearrange("p (a b) -> p a b", b=32),
                    ki.rearrange("p (a b) -> p a b", b=32),
                    bass.AP(f_c1, so, [[1280, 128], [1, nb], [0, 32]]), ALU.mult))
                self.w("sy", "V", vout[oc])
                gt_[oc] = self.dmaq(2 + p, dst[128 * oc:128 * (oc + 1), :], stage,
                                    queue="ge")
            self.fl()
        dequant(di["wup"], d_wdq[0], 24, 1024)
        self.tap("dwq", d_wdq[0][0:128, :])
        dequant(di["wgate"], d_wdq[1], 24, 1024)
        dequant(di["wdown"], d_wdqd, 8, 3072)
        self.tap("wdq", b_st[:, 0:1024])

        ge.memset(f_m[:, M_RM0:M_RM0 + 2], 0.0)
        self.bar()
        for gi, (d_raw, runc) in enumerate([(d_up, M_RM0), (d_gate, M_RM1)]):
            for ic in range(8):
                self.dma(wa[:, 3072 * ic:3072 * (ic + 1)],
                         d_wdq[gi][:, 128 * ic:128 * (ic + 1)], transpose=True)
            self.fl()
            if gi == 0:
                self.tap("wat", wa[:, 0:1024])
            lt = {}
            vq = {}
            ttag = {}
            atag = {}
            vred = {}
            gtag = {}
            ptag = {}
            h2bufs = [b_k, b_q]
            h2qbufs = [b_xb, b_h1]
            for tt in range(NT):
                tcols = slice(TT * tt, TT * (tt + 1))
                p = tt % 2
                self.w("sy", "V", ptag.get(tt - 2, 0))
                lt[tt] = self.dmaq(
                    p, h2bufs[p][:, :].rearrange("p (ic c) -> p ic c", c=TT),
                    d_h2[:, tcols].rearrange("(ic p) c -> p ic c", p=128))
                k0 = i_carve[:, 4096 * p:4096 * (p + 1)]
                self.w("ve", f"q{p}", lt[tt])
                ptag[tt] = self.vck(ve.tensor_scalar(
                    k0, h2bufs[p][:, :], f_m[:, R0:R0 + 1], None, ALU.mult))
                self.mc["P"] += 0
                self.w("ve", "T", ttag.get((tt - 2, 23), 0))
                vq[tt] = self.tag("V", ve.tensor_scalar(
                    h2qbufs[p][:, :], k0, f_m[:, S0:S0 + 1], None, ALU.mult))
                for i in range(24):
                    self.w("te", "V", vq[tt])
                    g4 = i // 4
                    self.w("te", "A", atag.get((tt, i - 2), 0) if i >= 2 else
                           atag.get((tt - 1, 22 + i), 0))
                    bank = PS[i % 2]
                    for ic in range(8):
                        last = te.matmul(
                            bank[:, :],
                            wa[:, 3072 * ic + 128 * i:3072 * ic + 128 * (i + 1)],
                            h2qbufs[p][:, 512 * ic:512 * (ic + 1)],
                            start=(ic == 0), stop=(ic == 7))
                    ttag[(tt, i)] = self.tag("T", last)
                    stg = b_st[:, 2048 * (g4 % 2) + 512 * (i % 4):
                               2048 * (g4 % 2) + 512 * (i % 4) + 512]
                    self.w("se", "T", ttag[(tt, i)])
                    self.w("se", f"q{2 + g4 % 2}",
                           gtag.get((tt, g4 - 2), 0) if g4 >= 2 else
                           gtag.get((tt - 1, 4 + g4), 0))
                    self.w("se", "V", vred.get((tt, 4 * (g4 - 2) + 3), 0) if g4 >= 2
                           else vred.get((tt - 1, 4 * (4 + g4) + 3), 0))
                    atag[(tt, i)] = self.tag("A", se.copy(stg, bank[:, :]))
                    self.w("ve", "A", atag[(tt, i)])
                    vred[(tt, i)] = self.tag("V", ve.tensor_reduce(
                        f_c1[:, 32 + i:33 + i], stg, AX.X, ALU.max,
                        apply_absolute_value=True))
                    if gi == 0 and tt == 1 and i == 3:
                        self.tap("stg1", b_st[:, 0:2048])
                    if i % 4 == 3:
                        oc0 = g4 * 4
                        self.w("sy", "A", atag[(tt, i)])
                        for o4 in range(4):
                            gtag[(tt, g4)] = self.dmaq(
                                2 + g4 % 2,
                                d_raw[oc0 + o4][:, tcols],
                                b_st[:, 2048 * (g4 % 2) + 512 * o4:
                                     2048 * (g4 % 2) + 512 * (o4 + 1)], queue="ge")
                self.w("ve", "V", vred[(tt, 23)])
                self.vck(ve.tensor_reduce(f_c1[:, 0:1], f_c1[:, 32:56], AX.X,
                                          ALU.max))
                self.vck(ve.tensor_tensor(f_m[:, runc:runc + 1],
                                          f_m[:, runc:runc + 1], f_c1[:, 0:1],
                                          ALU.max))
            self.fl()
        self.tap("dup", d_up[0][:, 0:1024])
        self.tap("dgt", d_gate[0][:, 0:1024])
        self.tap("dup7", d_up[23][:, T - 1024:T])
        if PHASES < 6:
            return self.finish()

        mup, mgt = fold_max([M_RM0, M_RM1], 1)
        qscale(mup, M_SCAL + 2, 2)
        qscale(mgt, M_SCAL + 4, 4)
        S1U, R1U, S1G, R1G = M_SCAL + 2, M_SCAL + 3, M_SCAL + 4, M_SCAL + 5
        self.tap("sc1", f_m[:, 72:80])

        ge.memset(f_m[:, M_RM0:M_RM0 + 1], 0.0)
        ve.tensor_scalar(f_c1[0:1, 34:35], f_m[0:1, S1G:S1G + 1], 1.0 / 65536.0, None,
                         ALU.mult)
        self.bar()
        bcast_scalar(f_c1[0:1, 34:35], 80, 6)
        C1G = 80
        ls = {}
        ptagk = {}
        ptagy = {}
        atag = {}
        vt2 = {}
        gtag = {}
        inbufs = [b_k, b_q]
        sbufs = [f_c0, f_P]
        for oc in range(24):
            p = oc % 2
            self.w("sy", "V", ptagk.get(oc - 2, 0))
            ls[oc] = self.dmaq(p, inbufs[p][:, :], d_gate[oc][:, :])
            kg = i_carve[:, 4096 * p:4096 * (p + 1)]
            yu = fxu[:, 4096 * p:4096 * (p + 1)]
            self.w("ve", f"q{p}", ls[oc])
            ptagk[oc] = self.vck(ve.tensor_scalar(
                kg, inbufs[p][:, :], f_m[:, R1G:R1G + 1], None, ALU.mult))
            self.w("se", "V", ptagk[oc])
            self.w("se", "A", ptagy.get(oc - 2, 0))
            self.ack(se.activation(
                sbufs[p][:, :], kg, AF.Sigmoid, scale=f_m[:, S1G:S1G + 1]))
            ptagy[oc] = self.tag("A", se.activation(
                yu, sbufs[p][:, :], AF.Identity, scale=cv(65536.0)))
            stage = b_v[:, 4096 * p:4096 * (p + 1)]
            self.w("ve", "A", ptagy[oc])
            self.w("ve", f"q{2 + p}", gtag.get(oc - 2, 0))
            self.vck(ve.scalar_tensor_tensor(stage, kg, f_m[:, C1G:C1G + 1], yu,
                                             ALU.mult, ALU.mult))
            vt2[oc] = self.tag("V", ve.tensor_reduce(
                f_c1[:, 32 + oc:33 + oc], stage, AX.X, ALU.max,
                apply_absolute_value=True))
            self.w("sy", "V", vt2[oc])
            gtag[oc] = self.dmaq(2 + p, d_t2[oc][:, :], stage, queue="ge")
        self.w("ve", "V", vt2[23])
        self.vck(ve.tensor_reduce(f_m[:, M_RM0:M_RM0 + 1], f_c1[:, 32:56], AX.X,
                                  ALU.max))
        self.fl()
        self.tap("t2", b_v[:, 0:1024])
        if PHASES < 7:
            return self.finish()

        mt2 = fold_max([M_RM0], 2)[0]
        qscale(mt2, 81, 7)
        S2, R2 = 81, 82
        ve.tensor_tensor(f_c1[0:1, 34:35], f_m[0:1, S2:S2 + 1],
                         f_m[0:1, S1U:S1U + 1], ALU.mult)
        self.bar()
        bcast_scalar(f_c1[0:1, 34:35], 83, 9)
        C2 = 83

        ge.memset(f_m[:, M_RM0:M_RM0 + 1], 0.0)
        self.bar()
        for oc in range(24):
            self.dma(wa[:, 1024 * oc:1024 * (oc + 1)],
                     d_wdqd[:, 128 * oc:128 * (oc + 1)], transpose=True)
        ls = {}
        ls2 = {}
        ptag = {}
        vop = {}
        gtag = {}
        t2bufs = [b_k, b_h1]
        upbufs = [b_q, b_xb]
        for oc in range(24):
            p = oc % 2
            self.w("sy", "A", ptag.get(oc - 2, 0))
            ls[oc] = self.dmaq(p, t2bufs[p][:, :], d_t2[oc][:, :])
            ls2[oc] = self.dmaq(4 + p, upbufs[p][:, :], d_up[oc][:, :])
            k3 = i_carve[:, 4096 * p:4096 * (p + 1)]
            k2 = fxi[:, 4096 * p:4096 * (p + 1)]
            self.w("se", f"q{p}", ls[oc])
            self.w("se", f"q{4 + p}", ls2[oc])
            se.activation(k3, t2bufs[p][:, :], AF.Identity, scale=f_m[:, R2:R2 + 1])
            ptag[oc] = self.tag("A", se.activation(
                k2, upbufs[p][:, :], AF.Identity, scale=f_m[:, R1U:R1U + 1]))
            stage = b_v[:, 4096 * p:4096 * (p + 1)]
            self.w("ve", "A", ptag[oc])
            self.w("ve", f"q{2 + p}", gtag.get(oc - 2, 0))
            self.vck(ve.scalar_tensor_tensor(stage, k3, f_m[:, C2:C2 + 1], k2,
                                             ALU.mult, ALU.mult))
            vop[oc] = self.tag("V", ve.tensor_reduce(
                f_c1[:, 32 + oc:33 + oc], stage, AX.X, ALU.max,
                apply_absolute_value=True))
            self.w("sy", "V", vop[oc])
            gtag[oc] = self.dmaq(2 + p, d_opre[oc][:, :], stage, queue="ge")
        self.w("ve", "V", vop[23])
        self.vck(ve.tensor_reduce(f_m[:, M_RM0:M_RM0 + 1], f_c1[:, 32:56], AX.X,
                                  ALU.max))
        self.fl()

        if PHASES < 8:
            return self.finish()

        mop = fold_max([M_RM0], 3)[0]
        qscale(mop, 84, 10)
        S3, R3 = 84, 85

        self.fl()
        oqbufs = [b_k, b_q, b_h1]
        oqcarv = [i_carve[:, 0:4096], i_carve[:, 4096:8192],
                  bsti[:, 0:4096]]

        def oq_ap(oc):
            return oqbufs[oc // 8][:, 512 * (oc % 8):512 * (oc % 8) + 512]
        lx = {}
        lo = {}
        ptag = {}
        voq = {}
        ttag = {}
        vout = {}
        stout = {}
        for tt in range(NT):
            tcols = slice(TT * tt, TT * (tt + 1))
            self.w("sy", "T", ttag.get((tt - 1, 7), 0))
            self.w("sy", "V", vout.get((tt - 1, 7), 0))
            pq = tt % 2
            lx[tt] = self.dmaq(pq, f_x[:, :].rearrange("p (ic c) -> p ic c", c=TT),
                               d_xmid[:, tcols].rearrange("(ic p) c -> p ic c", p=128))
            for g3 in range(3):
                lo[(tt, g3)] = self.dmaq(
                    pq, oqbufs[g3][:, :].rearrange("p (o c) -> p o c", c=512),
                    d_opre[8 * g3:8 * (g3 + 1)][:, :, tcols].rearrange(
                        "o p c -> p o c"))
            for g3 in range(3):
                self.w("se", f"q{pq}", lo[(tt, 2)])
                ptag[(tt, g3)] = self.tag("A", se.activation(
                    oqcarv[g3], oqbufs[g3][:, :], AF.Identity,
                    scale=f_m[:, R3:R3 + 1]))
                self.w("ve", "A", ptag[(tt, g3)])
                voq[(tt, g3)] = self.tag("V", ve.tensor_scalar(
                    oqbufs[g3][:, :], oqcarv[g3], f_m[:, S3:S3 + 1], None, ALU.mult))
            for jc in range(8):
                bank = PS[jc % 2]
                for g3 in range(3):
                    self.w("te", "V", voq[(tt, g3)])
                self.w("te", "V", vout.get((tt, jc - 2), 0) if jc >= 2 else
                       vout.get((tt - 1, 6 + jc), 0))
                for oc in range(24):
                    last = te.matmul(
                        bank[:, :],
                        wa[:, 1024 * oc + 128 * jc:1024 * oc + 128 * (jc + 1)],
                        oq_ap(oc), start=(oc == 0), stop=(oc == 23))
                ttag[(tt, jc)] = self.tag("T", last)
                stg = f_c0[:, 512 * (jc % 2):512 * (jc % 2) + 512]
                self.w("ve", "T", ttag[(tt, jc)])
                self.w("ve", f"q{pq}", lo[(tt, 2)])
                self.w("ve", f"q{6 + jc % 2}",
                       stout.get((tt, jc - 2), 0) if jc >= 2 else
                       stout.get((tt - 1, 6 + jc), 0))
                vout[(tt, jc)] = self.tag("V", ve.scalar_tensor_tensor(
                    stg, bank[:, :], f_m[:, M_GB + jc:M_GB + jc + 1],
                    f_x[:, 512 * jc:512 * (jc + 1)], ALU.mult, ALU.add))
                self.w("sy", "V", vout[(tt, jc)])
                stout[(tt, jc)] = self.dmaq(6 + jc % 2,
                                            outT[128 * jc:128 * (jc + 1), tcols], stg)
        return self.finish()

    def finish(self):
        self.fl()
        return self.nc


def build(taps=()):
    b = Bld(taps)
    with b.ctx:
        nc = b.build()
    return nc


def kernel(**inputs):
    x = np.asarray(inputs["L0_attn_hidden_states"], np.float32)
    c = np.asarray(inputs["L0_c"], np.float32)
    bgk2 = np.asarray(inputs["L0_attn_bgk2"], np.float32)
    rmsw = np.asarray(inputs["L0_attn_rms_w"], np.float32)

    base = {
        "Wq": np.ascontiguousarray(inputs["L0_attn_Wq"], np.float32),
        "Wk": np.ascontiguousarray(inputs["L0_attn_Wk"], np.float32),
        "Wv": np.ascontiguousarray(inputs["L0_attn_Wv"], np.float32),
        "Wg": np.ascontiguousarray(inputs["L0_attn_Wg"], np.float32),
        "Wgk1": np.ascontiguousarray(inputs["L0_attn_Wgk1"], np.float32),
        "Wgk2": np.ascontiguousarray(inputs["L0_attn_Wgk2"], np.float32),
        "Wo": np.ascontiguousarray(inputs["L0_attn_Wo"], np.float32),
        "wup": np.ascontiguousarray(inputs["L0_ffn_w_up"], np.float32),
        "wgate": np.ascontiguousarray(inputs["L0_ffn_w_gate"], np.float32),
        "wdown": np.ascontiguousarray(inputs["L0_ffn_w_down"], np.float32),
        "wmod": np.ascontiguousarray(inputs["L0_wmod"], np.float32),
        "bgk2f": np.ascontiguousarray(bgk2.reshape(8, 128).T),
        "rmsw": np.ascontiguousarray(rmsw.reshape(1, 512)),
    }
    nc = build(DEBUG_TAPS)
    in_maps = []
    for core in range(8):
        b_, s_ = divmod(core, 2)
        m = dict(base)
        m["xT"] = np.ascontiguousarray(x[b_, 4096 * s_:4096 * (s_ + 1), :].T)
        m["cfm"] = np.ascontiguousarray(c[b_].reshape(8, 128).T)
        sel = np.zeros((1, 8), np.float32)
        if s_ == 1:
            sel[0, core - 1] = 1.0
        m["selS"] = sel
        oh = np.zeros((1, 8), np.float32)
        oh[0, core] = 1.0
        m["oneh"] = oh
        in_maps.append(m)
    res = run_bass_kernel_spmd(nc, in_maps, core_ids=list(range(8)))
    out = np.empty((4, 8192, 1024), np.float32)
    for core in range(8):
        b_, s_ = divmod(core, 2)
        out[b_, 4096 * s_:4096 * (s_ + 1), :] = res.results[core]["outT"].T
    kernel._results = res.results
    return out


# version 14
# speedup vs baseline: 1.0784x; 1.0784x over previous
import numpy as np
import ml_dtypes
from contextlib import ExitStack

import concourse.bass as bass
import concourse.mybir as mybir
from concourse.bass_utils import run_bass_kernel_spmd

F32 = mybir.dt.float32
BF16 = mybir.dt.bfloat16
I16 = mybir.dt.int16
U16 = mybir.dt.uint16
U8 = mybir.dt.uint8
AF = mybir.ActivationFunctionType
ALU = mybir.AluOpType
AX = mybir.AxisListType

D = 1024
T = 4096
TT = 512
NT = T // TT
H = 4
DV = 2048
SC = 128
NSC = T // SC
FF = 3072
LN16 = float(np.log(16.0))
QMAX = 32767.0

DEBUG_TAPS = set()
PHASES = 99
BP = 99


class Bld:
    def __init__(self, taps=()):
        self.nc = bass.Bass("TRN2", target_bir_lowering=False, dynamic_dma_scratch_size=8192)
        self.ctx = ExitStack()
        self.taps = set(taps)
        self.tapn = {}
        self.dctr = 0

    def sb(self, name, shape, dt):
        return self.ctx.enter_context(self.nc.sbuf_tensor(name, shape, dt))

    def ps(self, name, shape, dt=F32):
        return self.ctx.enter_context(self.nc.psum_tensor(name, shape, dt))

    ENGS = ("te", "ve", "se", "ge", "sy")

    def bar(self):
        self.nc.all_engine_barrier()
        for e in self.ENGS:
            for k in self.mc:
                self.lastw[(e, k)] = self.mc[k]

    def eng(self, name):
        return {"te": self.nc.tensor, "ve": self.nc.vector, "se": self.nc.scalar,
                "ge": self.nc.gpsimd, "sy": self.nc.sync}[name]

    def tag(self, key, inst):
        inst.then_inc(self.sems[key], 1)
        self.mc[key] += 1
        return self.mc[key]

    def w(self, ename, key, val):
        if val is None or val <= 0:
            return
        if self.lastw.get((ename, key), 0) >= val:
            return
        self.eng(ename).wait_ge(self.sems[key], val)
        self.lastw[(ename, key)] = val

    def dma(self, out, in_, transpose=False):
        self.nc.sync.dma_start(out=out, in_=in_, transpose=transpose).then_inc(
            self.dsem, 16)
        self.dctr += 16
        self.mc["S"] = self.dctr
        return self.dctr

    def dmag(self, out, in_):
        self.nc.gpsimd.dma_start(out=out, in_=in_).then_inc(self.gsem, 16)
        self.gctr += 16
        self.mc["G"] = self.gctr
        return self.gctr

    def vck(self, inst):
        v = self.tag("V", inst)
        self.nc.vector.wait_ge(self.sems["V"], v)
        self.lastw[("ve", "V")] = v
        return v

    def ack(self, inst):
        v = self.tag("A", inst)
        self.nc.scalar.wait_ge(self.sems["A"], v)
        self.lastw[("se", "A")] = v
        return v

    def dmaq(self, qi, out, in_, transpose=False, queue="sy"):
        eng = self.nc.sync
        k = f"q{qi}"
        eng.dma_start(out=out, in_=in_, transpose=transpose).then_inc(self.sems[k], 16)
        self.mc[k] += 16
        return self.mc[k]

    def fl(self):
        self.nc.sync.wait_ge(self.dsem, self.dctr)
        for i in range(8):
            k = f"q{i}"
            if self.mc[k]:
                self.nc.sync.wait_ge(self.sems[k], self.mc[k])
        if self.gctr:
            self.nc.gpsimd.wait_ge(self.gsem, self.gctr)
        self.bar()

    def tap(self, name, ap):
        if name not in self.taps:
            return
        k = self.tapn.get(name, 0)
        self.tapn[name] = k + 1
        t = self.nc.dram_tensor(f"tap_{name}_{k}", list(ap.shape), ap.dtype,
                                kind="ExternalOutput")
        idx = tuple(slice(0, s) for s in ap.shape)
        self.dma(t[idx], ap)
        self.fl()

    def pingpong(self, n, emit_mms, emit_evac, banks):
        nc = self.nc
        nb = len(banks)
        pe0 = self.ppe_ctr
        ev0 = self.pev_ctr
        for i in range(n):
            if i >= nb:
                nc.tensor.wait_ge(self.sev, ev0 + i - nb + 1)
            emit_mms(i, banks[i % nb]).then_inc(self.spe, 1)
            self.ppe_ctr += 1
        for i in range(n):
            nc.scalar.wait_ge(self.spe, pe0 + i + 1)
            emit_evac(i, banks[i % nb]).then_inc(self.sev, 1)
            self.pev_ctr += 1
        self.bar()

    def build(self):
        nc = self.nc
        di = {}
        for name, shape in [
            ("xT", [D, T]), ("Wq", [D, D]), ("Wk", [D, D]), ("Wv", [D, DV]),
            ("Wg", [D, DV]), ("Wgk1", [D, 16]), ("Wgk2", [16, D]),
            ("bgk2f", [128, 8]), ("rmsw", [1, 512]), ("Wo", [DV, D]),
            ("wup", [FF, D]), ("wgate", [FF, D]), ("wdown", [D, FF]),
            ("wmod", [D, 6 * D]), ("cfm", [128, 8]), ("selS", [1, 8]),
            ("oneh", [1, 8]),
        ]:
            di[name] = nc.dram_tensor(name, shape, F32, kind="ExternalInput")
        outT = nc.dram_tensor("outT", [D, T], F32, kind="ExternalOutput")

        d_h1 = nc.dram_tensor("d_h1", [D, T], BF16)
        d_h2 = nc.dram_tensor("d_h2", [D, T], BF16)
        d_xmid = nc.dram_tensor("d_xmid", [D, T], F32)
        d_oloc = nc.dram_tensor("d_oloc", [NSC, H, 128, 512], BF16)
        d_qc = nc.dram_tensor("d_qc", [NSC, D, 128], BF16)
        d_kh = nc.dram_tensor("d_kh", [2, D, 128], BF16)
        d_sin = nc.dram_tensor("d_sin", [D, 512], BF16)
        d_sout = nc.dram_tensor("d_sout", [8 * D, 512], BF16)
        d_up = nc.dram_tensor("d_up", [24, 128, T], BF16)
        d_gate = nc.dram_tensor("d_gate", [24, 128, T], BF16)
        d_t2 = nc.dram_tensor("d_t2", [24, 128, T], BF16)
        d_opre = nc.dram_tensor("d_opre", [24, 128, T], BF16)
        d_wdq = nc.dram_tensor("d_wdq", [2, FF, D], BF16)
        d_wdqd = nc.dram_tensor("d_wdqd", [D, FF], BF16)
        d_scal = nc.dram_tensor("d_scal", [1, 64], F32)
        d_pmax = nc.dram_tensor("d_pmax", [128, 1], F32)
        ar_io = [(nc.dram_tensor(f"d_arin{k}", [1, 16], F32),
                  nc.dram_tensor(f"d_arout{k}", [1, 16], F32)) for k in range(4)]
        mask_np = np.triu(np.ones((128, 128), np.float32)).astype(ml_dtypes.bfloat16)
        d_mask = nc.inline_tensor(mask_np, "d_mask")

        WAC = 32896
        wa = self.sb("wa", [128, WAC], BF16)
        f_x = self.sb("f_x", [128, 4096], F32)
        f_P = self.sb("f_P", [128, 4096], F32)
        f_S = self.sb("f_S", [128, 4096], F32)
        f_c0 = self.sb("f_c0", [128, 4096], F32)
        f_c1 = self.sb("f_c1", [128, 1280], F32)
        f_m = self.sb("f_m", [128, 96], F32)
        cp = self.sb("cp", [128, 10], F32)
        cm1 = self.sb("cm1", [128, 1], F32)
        b_q = self.sb("b_q", [128, 4096], BF16)
        b_k = self.sb("b_k", [128, 4096], BF16)
        b_v = self.sb("b_v", [128, 8192], BF16)
        b_h1 = self.sb("b_h1", [128, 4096], BF16)
        b_xb = self.sb("b_xb", [128, 4096], BF16)
        b_e = self.sb("b_e", [128, 2048], BF16)
        b_sbf = self.sb("b_sbf", [128, 4096], BF16)
        b_st = self.sb("b_st", [128, 4096], BF16)
        b_kh = self.sb("b_kh", [128, 1024], BF16)
        b_ms = self.sb("b_ms", [128, 128], BF16)
        b_rw = self.sb("b_rw", [1, 512], BF16)
        b_on = self.sb("b_on", [128, 1], BF16)
        b_or = self.sb("b_or", [1, 128], BF16)
        wgk2 = self.sb("wgk2", [16, 1024], BF16)
        bgk2 = self.sb("bgk2", [128, 16], F32)
        u8m = self.sb("u8m", [128, 8], U8)
        i_carve = f_S.bitcast(I16)
        u_carve = f_S.bitcast(U16)
        PS = [self.ps(f"PS{i}", [128, 512]) for i in range(8)]

        self.dsem = self.ctx.enter_context(nc.semaphore("dS"))
        self.gsem = self.ctx.enter_context(nc.semaphore("dG"))
        self.spe = self.ctx.enter_context(nc.semaphore("sPE"))
        self.sev = self.ctx.enter_context(nc.semaphore("sEV"))
        ccsem = self.ctx.enter_context(nc.semaphore("sCC"))
        self.sems = {k: self.ctx.enter_context(nc.semaphore("m" + k))
                     for k in ("T", "V", "A", "P")}
        self.sems["S"] = self.dsem
        self.sems["G"] = self.gsem
        for i in range(8):
            self.sems[f"q{i}"] = self.ctx.enter_context(nc.semaphore(f"q{i}"))
        self.mc = {"T": 0, "V": 0, "A": 0, "P": 0, "S": 0, "G": 0}
        for i in range(8):
            self.mc[f"q{i}"] = 0
        self.lastw = {}
        self.gctr = 0
        self.ppe_ctr = 0
        self.pev_ctr = 0
        cc = [0]

        te, ve, se, ge = nc.tensor, nc.vector, nc.scalar, nc.gpsimd

        CONSTS = [-LN16, -1.0 / 16.0, 1.0 / 16.0, -0.5, 1e-5, 1.0 / 1024.0,
                  1.0 / 512.0, 1.0, 65536.0]
        for j, v in enumerate(CONSTS):
            ge.memset(cp[:, j:j + 1], float(v))
        ge.memset(cm1[:, :], -1.0)

        def cv(v):
            return cp[:, CONSTS.index(v):CONSTS.index(v) + 1]

        def cv1(v):
            return cp[0:1, CONSTS.index(v):CONSTS.index(v) + 1]

        M_SA, M_CA1, M_GA, M_SB, M_CB1, M_GB = 0, 8, 16, 24, 32, 40
        M_EXPG, M_EXPL, M_LAM = 48, 56, 64
        M_RM0, M_RM1 = 72, 73
        M_SCAL = 74

        ge.memset(f_m[:, M_LAM:M_LAM + 8], 0.0)
        ge.memset(f_m[:, M_RM0:M_RM0 + 2], 0.0)
        ge.memset(b_on[:, :], 1.0)
        ge.memset(b_or[:, :], 1.0)
        ones_col = b_on[:, 0:1]
        ones_row = b_or[0:1, :]

        self.dma(b_ms[:, :], d_mask[:, :])
        self.dma(f_m[:, M_SA:M_SA + 8], di["cfm"][:, :])
        self.dma(f_c1[0:1, 0:512], di["rmsw"][:, :])
        self.fl()
        ve.tensor_copy(b_rw[:, :], f_c1[0:1, 0:512])
        se.activation(f_m[:, M_SA:M_SA + 8], f_m[:, M_SA:M_SA + 8], AF.Silu)
        self.bar()

        siluc = f_m[:, M_SA:M_SA + 8]
        ge.memset(f_c0[:, 0:48], 0.0)
        self.bar()
        for ic in range(8):
            for hh in range(2):
                self.dma(f_x[:, 0:3072], di["wmod"][128 * ic:128 * (ic + 1),
                                                    3072 * hh:3072 * (hh + 1)])
                self.fl()
                for j24 in range(24):
                    jc = 24 * hh + j24
                    te.matmul(PS[0][:, jc:jc + 1], f_x[:, 128 * j24:128 * (j24 + 1)],
                              siluc[:, ic:ic + 1], start=True, stop=True)
                self.bar()
            se.copy(f_c1[:, 16:64], PS[0][:, 0:48])
            self.bar()
            ve.tensor_tensor(f_c0[:, 0:48], f_c0[:, 0:48], f_c1[:, 16:64], ALU.add)
            self.bar()
        ve.tensor_copy(f_m[:, 0:48], f_c0[:, 0:48])
        self.bar()
        ve.tensor_scalar(f_m[:, M_CA1:M_CA1 + 8], f_m[:, M_CA1:M_CA1 + 8], 1.0, None,
                         ALU.add)
        ve.tensor_scalar(f_m[:, M_CB1:M_CB1 + 8], f_m[:, M_CB1:M_CB1 + 8], 1.0, None,
                         ALU.add)
        self.bar()
        for off in (M_GA, M_GB):
            g = f_m[:, off:off + 8]
            ve.tensor_scalar(u8m[:, :], g, 0.0, None, ALU.is_ge)
            ve.tensor_scalar(f_c1[:, 0:8], g, 0.25, None, ALU.max)
            ve.tensor_scalar(f_c1[:, 8:16], g, -0.25, None, ALU.min)
            self.bar()
            ve.select(g, u8m[:, :], f_c1[:, 0:8], f_c1[:, 8:16])
            self.bar()
        self.tap("mod", f_m[:, 0:48])
        if PHASES < 1:
            return self.finish()

        WQ, WK, WV, GK1 = 0, 8192, 16384, 32768

        def load_w(dst_off, src, ic_cols, n_ic):
            for ic in range(n_ic):
                self.dma(f_x[:, 0:ic_cols], src[128 * ic:128 * (ic + 1), :])
                self.fl()
                ve.tensor_copy(
                    wa[:, dst_off + ic * ic_cols:dst_off + (ic + 1) * ic_cols],
                    f_x[:, 0:ic_cols])
                self.bar()

        load_w(WQ, di["Wq"], 1024, 8)
        load_w(WK, di["Wk"], 1024, 8)
        load_w(WV, di["Wv"], 2048, 8)
        load_w(GK1, di["Wgk1"], 16, 8)
        self.dma(f_c1[0:16, 0:1024], di["Wgk2"][:, :])
        self.fl()
        ve.tensor_copy(wgk2[:, :], f_c1[0:16, 0:1024])
        self.bar()
        self.dma(f_c1[:, 0:8], di["bgk2f"][:, :])
        self.fl()
        ve.tensor_copy(bgk2[:, 0:8], f_c1[:, 0:8])
        ve.tensor_scalar(bgk2[:, 8:16], f_c1[:, 0:8], -1.0, None, ALU.mult)
        self.bar()
        ge.memset(f_S[:, :], 0.0)
        ge.memset(b_sbf[:, :], 0.0)
        self.bar()

        def ln_block(src_b, sq_b, sa_off, ca1_off, out_b):
            se.activation(sq_b[:, :], src_b[:, :], AF.Square)
            self.bar()
            for ic in range(8):
                te.matmul(PS[2][0:1, :], ones_col, src_b[:, 512 * ic:512 * (ic + 1)],
                          start=(ic == 0), stop=(ic == 7))
            for ic in range(8):
                te.matmul(PS[3][0:1, :], ones_col, sq_b[:, 512 * ic:512 * (ic + 1)],
                          start=(ic == 0), stop=(ic == 7))
            self.bar()
            st = f_c1[0:1, :]
            ve.tensor_scalar(st[:, 0:512], PS[2][0:1, :], 1.0 / 1024.0, None, ALU.mult)
            ve.tensor_scalar(st[:, 512:1024], PS[3][0:1, :], 1.0 / 1024.0, None,
                             ALU.mult)
            self.bar()
            ve.tensor_tensor(f_c0[0:1, 0:512], st[:, 0:512], st[:, 0:512], ALU.mult)
            self.bar()
            ve.tensor_tensor(st[:, 512:1024], st[:, 512:1024], f_c0[0:1, 0:512],
                             ALU.subtract)
            self.bar()
            se.activation(st[:, 512:1024], st[:, 512:1024], AF.Ln, bias=cv1(1e-5))
            self.bar()
            se.activation(st[:, 512:1024], st[:, 512:1024], AF.Exp, scale=cv1(-0.5))
            self.bar()
            ve.tensor_tensor(st[:, 0:512], st[:, 0:512], st[:, 512:1024], ALU.mult)
            self.bar()
            rstd_b = b_e[0:1, 0:512]
            mrstd_b = b_e[0:1, 512:1024]
            ve.tensor_copy(rstd_b, st[:, 512:1024])
            ve.tensor_copy(mrstd_b, st[:, 0:512])
            self.bar()
            te.matmul(PS[4][:, :], ones_row, rstd_b, start=True, stop=True)
            te.matmul(PS[5][:, :], ones_row, mrstd_b, start=True, stop=True)
            self.bar()
            for ic in range(8):
                o = (ic % 4) * 1024
                dst = (f_c0 if ic < 4 else f_P)
                ve.tensor_tensor(dst[:, o:o + 512], src_b[:, 512 * ic:512 * (ic + 1)],
                                 PS[4][:, :], ALU.mult)
            self.bar()
            for ic in range(8):
                o = (ic % 4) * 1024
                dst = (f_c0 if ic < 4 else f_P)
                ve.tensor_tensor(dst[:, o:o + 512], dst[:, o:o + 512], PS[5][:, :],
                                 ALU.subtract)
            self.bar()
            for ic in range(8):
                o = (ic % 4) * 1024
                dst = (f_c0 if ic < 4 else f_P)
                se.activation(out_b[:, 512 * ic:512 * (ic + 1)], dst[:, o:o + 512],
                              AF.Identity, bias=f_m[:, sa_off + ic:sa_off + ic + 1],
                              scale=f_m[:, ca1_off + ic:ca1_off + ic + 1])
            self.bar()

        for tt in range(NT):
            tcols = slice(TT * tt, TT * (tt + 1))
            self.dma(f_x[:, :].rearrange("p (ic c) -> p ic c", c=TT),
                     di["xT"][:, tcols].rearrange("(ic p) c -> p ic c", p=128))
            self.fl()
            ve.tensor_copy(b_xb[:, :], f_x[:, :])
            self.bar()
            ln_block(b_xb, b_h1, M_SA, M_CA1, b_h1)
            self.tap("h1", b_h1[:, 0:1024])
            self.dma(d_h1[:, tcols].rearrange("(ic p) c -> p ic c", p=128),
                     b_h1[:, :].rearrange("p (ic c) -> p ic c", c=TT))
            if PHASES < 2:
                self.fl()
                continue

            def qk(w_off, out_sb):
                def mms(i, bank):
                    for ic in range(8):
                        last = te.matmul(
                            bank[:, :],
                            wa[:, w_off + ic * 1024 + 128 * i:
                               w_off + ic * 1024 + 128 * (i + 1)],
                            b_h1[:, 512 * ic:512 * (ic + 1)],
                            start=(ic == 0), stop=(ic == 7))
                    return last

                def evac(i, bank):
                    return se.copy(out_sb[:, 512 * i:512 * (i + 1)], bank[:, :])
                self.pingpong(8, mms, evac, [PS[0], PS[1]])
            qk(WQ, b_q)
            qk(WK, b_k)
            self.tap("q", b_q[:, 0:1024])
            self.tap("k", b_k[:, 0:1024])

            def v_mms(i, bank):
                tc, eg = divmod(i, 4)
                for ic in range(8):
                    last = te.matmul(
                        bank[:, :],
                        b_h1[:, 512 * ic + 128 * tc:512 * ic + 128 * (tc + 1)],
                        wa[:, WV + ic * 2048 + 512 * eg:WV + ic * 2048 + 512 * (eg + 1)],
                        start=(ic == 0), stop=(ic == 7))
                return last

            def v_evac(i, bank):
                tc, eg = divmod(i, 4)
                return se.copy(
                    b_v[:, tc * 2048 + 512 * eg:tc * 2048 + 512 * (eg + 1)], bank[:, :])
            self.pingpong(16, v_mms, v_evac, [PS[0], PS[1]])
            self.tap("v", b_v[:, 0:2048])

            for ic in range(8):
                te.matmul(PS[0][0:16, :], wa[:, GK1 + 16 * ic:GK1 + 16 * (ic + 1)],
                          b_h1[:, 512 * ic:512 * (ic + 1)], start=(ic == 0),
                          stop=(ic == 7))
            self.bar()
            t1b = b_e[0:16, 1024:1536]
            se.copy(t1b, PS[0][0:16, :])
            self.bar()
            for r in range(4):
                for half in range(2):
                    ic = 2 * r + half
                    te.matmul(PS[3 + half][:, :], wgk2[:, 128 * ic:128 * (ic + 1)],
                              t1b, start=True, stop=True)
                self.bar()
                u0 = f_c0[:, 0:1024]
                aa = f_c0[:, 1024:2048]
                bb = f_c0[:, 2048:3072]
                for half in range(2):
                    ic = 2 * r + half
                    se.activation(u0[:, 512 * half:512 * (half + 1)],
                                  PS[3 + half][:, :], AF.Identity,
                                  bias=bgk2[:, ic:ic + 1])
                self.bar()
                se.activation(aa, u0, AF.Abs)
                self.bar()
                se.activation(bb, aa, AF.Exp, scale=cm1[:, 0:1])
                self.bar()
                se.activation(aa, bb, AF.Ln, bias=cv(1.0))
                self.bar()
                ve.tensor_scalar(bb, u0, -1.0, 0.0, ALU.mult, ALU.max)
                self.bar()
                ve.tensor_tensor(f_P[:, 1024 * r:1024 * (r + 1)], aa, bb, ALU.add)
                self.bar()
            self.tap("P", f_P[:, 0:1024])
            if PHASES < 3:
                continue

            scst = getattr(self, "_scst", None)
            if scst is None:
                scst = self._scst = {"pm": 0, "ee": 0, "kq": 0, "mk": 0, "oh": 0,
                                     "evA": 0, "sb1": 0, "stt": 0, "lam": 0,
                                     "kw": {}, "kr": {}, "qout": {}, "osp": {}}
            for s4 in range(4):
                sc = 4 * tt + s4
                co = 128 * s4
                pq = sc % 2
                self.w("ve", "A", scst["ee"])
                for ic in range(8):
                    base = ic * 512 + co
                    last = ve.tensor_scalar(f_P[:, base + 64:base + 128],
                                            f_P[:, base + 64:base + 128],
                                            f_P[:, base + 63:base + 64], None,
                                            ALU.add)
                scst["pm"] = self.tag("V", last)
                pv = f_P[:, :].rearrange("p (ic c) -> p ic c", c=512)[:, :, co:co + 128]
                pg = f_P[:, :].rearrange("p (ic c) -> p ic c",
                                         c=512)[:, :, co + 127:co + 128]
                ep = b_e[:, 0:1024].rearrange("p (ic c) -> p ic c", c=128)
                em = b_e[:, 1024:2048].rearrange("p (ic c) -> p ic c", c=128)
                self.w("se", "V", scst["pm"])
                self.w("se", "V", scst["lam"])
                se.activation(ep, pv, AF.Exp, scale=cv(-1.0 / 16.0), bias=cv(-LN16))
                se.activation(em, pv, AF.Exp, scale=cv(1.0 / 16.0))
                se.activation(f_m[:, M_EXPG:M_EXPG + 8].rearrange(
                    "p (a b) -> p a b", b=1), pg, AF.Exp, scale=cv(-1.0 / 16.0))
                scst["ee"] = self.tag("A", se.activation(
                    f_m[:, M_EXPL:M_EXPL + 8], f_m[:, M_LAM:M_LAM + 8],
                    AF.Exp, scale=cv(-1.0 / 16.0)))
                qv = b_q[:, :].rearrange("p (dk c) -> p dk c", c=512)[:, :, co:co + 128]
                kv = b_k[:, :].rearrange("p (dk c) -> p dk c", c=512)[:, :, co:co + 128]
                self.w("ve", "A", scst["ee"])
                ve.tensor_tensor(qv, qv, ep, ALU.mult)
                self.vck(ve.tensor_tensor(kv, kv, em, ALU.mult))
                self.w("ve", f"q{4 + (1 - pq)}", scst["qout"].get(sc - 1, 0))
                self.w("ve", f"q{4 + pq}", scst["kr"].get(sc - 2, 0))
                ve.tensor_tensor(
                    b_st[:, 0:1024].rearrange("p (dc c) -> p dc c", c=128), kv,
                    bass.AP(f_m, M_EXPG, [[96, 128], [1, 8], [0, 128]]), ALU.mult)
                scst["kq"] = self.tag("V", ve.tensor_tensor(
                    b_st[:, 1024:2048].rearrange("p (dc c) -> p dc c", c=128), qv,
                    bass.AP(f_m, M_EXPL, [[96, 128], [1, 8], [0, 128]]), ALU.mult))
                self.w("sy", "V", scst["kq"])
                kw = self.dmaq(4 + pq, d_kh[pq].rearrange("(dc p) c -> p dc c", p=128),
                               b_st[:, 0:1024].rearrange("p (dc c) -> p dc c", c=128))
                scst["kw"][sc] = kw
                scst["qout"][sc] = self.dmaq(
                    4 + pq, d_qc[sc].rearrange("(dc p) c -> p dc c", p=128),
                    b_st[:, 1024:2048].rearrange("p (dc c) -> p dc c", c=128))
                self.w("sy", f"q{4 + pq}", scst["qout"][sc])
                self.w("sy", "T", scst.get("ut", 0))
                scst["kr"][sc] = self.dmaq(4 + pq, b_kh[:, :], d_kh[pq][:, :],
                                           transpose=True)
                self.w("te", "V", scst["kq"])
                for h in range(H):
                    for dc in range(2):
                        base = (2 * h + dc) * 512 + co
                        last = te.matmul(PS[2][:, 128 * h:128 * (h + 1)],
                                         b_k[:, base:base + 128],
                                         b_q[:, base:base + 128],
                                         start=(dc == 0), stop=(dc == 1))
                at = self.tag("T", last)
                self.w("ve", "T", at)
                scst["mk"] = self.tag("V", ve.tensor_tensor(
                    b_st[:, 2048:2560].rearrange("p (h c) -> p h c", c=128),
                    PS[2][:, :].rearrange("p (h c) -> p h c", c=128),
                    bass.AP(b_ms, 0, [[128, 128], [0, 4], [1, 128]]), ALU.mult))
                ohh = {}
                self.w("te", "V", scst["mk"])
                self.w("te", "A", scst["sb1"])
                for h in range(H):
                    pso = PS[3 + h]
                    for m4 in range(4):
                        oc = 128 * m4
                        for dc in range(2):
                            blk = 2 * h + dc
                            te.matmul(pso[:, oc:oc + 128],
                                      b_sbf[:, 512 * blk + oc:512 * blk + oc + 128],
                                      b_q[:, blk * 512 + co:blk * 512 + co + 128],
                                      start=(dc == 0), stop=False)
                        last = te.matmul(pso[:, oc:oc + 128],
                                         b_v[:, s4 * 2048 + 512 * h + oc:
                                            s4 * 2048 + 512 * h + oc + 128],
                                         b_st[:, 2048 + 128 * h:2048 + 128 * (h + 1)],
                                         start=False, stop=True)
                    ohh[h] = self.tag("T", last)
                scst["oh"] = ohh[3]
                for h in range(H):
                    stg = b_st[:, 2560 + 512 * (h % 2):3072 + 512 * (h % 2)]
                    self.w("se", "T", ohh[h])
                    self.w("se", f"q{6 + h % 2}", scst["osp"].get((sc, h - 2), 0)
                           if h >= 2 else scst["osp"].get((sc - 1, 2 + h), 0))
                    ea = self.tag("A", se.copy(stg, PS[3 + h][:, :]))
                    self.w("sy", "A", ea)
                    scst["osp"][(sc, h)] = self.dmaq(6 + h % 2, d_oloc[sc, h], stg)
                scst["evA"] = ea
                for rnd in range(2):
                    banks = [PS[0], PS[1], PS[7], PS[2]]
                    self.w("te", f"q{4 + pq}", scst["kr"][sc])
                    if rnd == 1:
                        self.w("te", "V", scst["stt"])
                    for j in range(4):
                        blk = 4 * rnd + j
                        h = blk // 2
                        last = te.matmul(banks[j][:, :],
                                         b_kh[:, 128 * blk:128 * (blk + 1)],
                                         b_v[:, s4 * 2048 + 512 * h:
                                            s4 * 2048 + 512 * (h + 1)],
                                         start=True, stop=True)
                    ut = self.tag("T", last)
                    scst["ut"] = ut
                    self.w("ve", "T", ut)
                    for j in range(4):
                        blk = 4 * rnd + j
                        last = ve.scalar_tensor_tensor(
                            f_S[:, 512 * blk:512 * (blk + 1)],
                            f_S[:, 512 * blk:512 * (blk + 1)],
                            f_m[:, M_EXPG + blk:M_EXPG + blk + 1],
                            banks[j][:, :], ALU.mult, ALU.add)
                    scst["stt"] = self.tag("V", last)
                    self.w("se", "V", scst["stt"])
                    for j in range(4):
                        blk = 4 * rnd + j
                        last = se.copy(b_sbf[:, 512 * blk:512 * (blk + 1)],
                                       f_S[:, 512 * blk:512 * (blk + 1)])
                    scst["sb1"] = self.tag("A", last)
                scst["lam"] = self.tag("V", ve.tensor_tensor(
                    f_m[:, M_LAM:M_LAM + 8], f_m[:, M_LAM:M_LAM + 8],
                    pg.rearrange("p a b -> p (a b)"), ALU.add))
            self.bar()
            self.tap("S", f_S[:, 0:1024])
        if PHASES < 4:
            return self.finish()

        self.dma(d_sin[:, :].rearrange("(blk p) c -> p blk c", p=128),
                 b_sbf[:, :].rearrange("p (blk c) -> p blk c", c=512))
        self.fl()
        ge.collective_compute("AllGather", ALU.bypass,
                              replica_groups=[list(range(8))],
                              ins=[d_sin[:, :]], outs=[d_sout[:, :]]).then_inc(ccsem)
        cc[0] += 1
        ge.wait_ge(ccsem, cc[0])
        self.bar()
        self.dma(f_c1[:, 0:8], bass.AP(di["selS"], 0, [[0, 128], [1, 8]]))
        self.fl()
        ge.memset(f_S[:, :], 0.0)
        self.bar()
        for r in range(8):
            self.dma(b_st[:, :].rearrange("p (blk c) -> p blk c", c=512),
                     d_sout[D * r:D * (r + 1), :].rearrange("(blk p) c -> p blk c",
                                                            p=128))
            self.fl()
            ve.scalar_tensor_tensor(f_S[:, :], b_st[:, :], f_c1[:, r:r + 1], f_S[:, :],
                                    ALU.mult, ALU.add)
            self.bar()
        ve.tensor_copy(b_sbf[:, :], f_S[:, :])
        self.bar()
        self.tap("Sin", b_sbf[:, 0:1024])

        WO, WGB = 0, 16384
        load_w(WO, di["Wo"], 1024, 16)
        load_w(WGB, di["Wg"], 2048, 8)

        for tt in range(NT):
            tcols = slice(TT * tt, TT * (tt + 1))
            self.dmaq(2, f_x[:, :].rearrange("p (ic c) -> p ic c", c=TT),
                      di["xT"][:, tcols].rearrange("(ic p) c -> p ic c", p=128))
            lxh = getattr(self, "_lxh", {})
            self._lxh = lxh
            lxh[tt] = self.dmaq(2, b_h1[:, :].rearrange("p (ic c) -> p ic c", c=TT),
                                d_h1[:, tcols].rearrange("(ic p) c -> p ic c", p=128))
            self.fl()

            def gt_mms(i, bank):
                for ic in range(8):
                    last = te.matmul(
                        bank[:, :],
                        wa[:, WGB + ic * 2048 + 128 * i:WGB + ic * 2048 + 128 * (i + 1)],
                        b_h1[:, 512 * ic:512 * (ic + 1)],
                        start=(ic == 0), stop=(ic == 7))
                return last

            def gt_evac(i, bank):
                return se.activation(b_v[:, 512 * i:512 * (i + 1)], bank[:, :], AF.Silu)
            self.pingpong(16, gt_mms, gt_evac, [PS[0], PS[1]])
            self.tap("sgt", b_v[:, 0:1024])

            corrbank = [PS[3], PS[5], PS[2], PS[6]]
            c1 = {}
            v1 = {}
            a1 = {}
            c2 = {}
            a2 = {}
            c3 = {}
            v2 = {}
            vstt = {}
            lsc = {}
            ofull = [f_c1[:, 0:512], f_c1[:, 512:1024],
                     f_c0[:, 0:512], f_c0[:, 512:1024]]
            qcb = [b_st[:, 0:1024], b_h1[:, 0:1024]]

            def sc_loads(s4):
                if s4 > 3:
                    return
                sc = 4 * tt + s4
                pp = s4 % 2
                for h in range(H):
                    self.dmaq(pp, b_k[:, 2048 * pp + 512 * h:2048 * pp + 512 * (h + 1)],
                              d_oloc[sc, h])
                lsc[s4] = self.dmaq(
                    pp, qcb[pp][:, :].rearrange("p (dc c) -> p dc c", c=128),
                    d_qc[sc].rearrange("(dc p) c -> p dc c", p=128))
            sc_loads(0)
            sc_loads(1)
            for s4 in range(4):
                sc = 4 * tt + s4
                pp = s4 % 2
                for h in range(H):
                    self.w("te", f"q{pp}", lsc[s4])
                    self.w("te", "A", a2.get((s4 - 1, h), 0))
                    self.w("te", "V", v1.get((s4 - 1, h), 0))
                    for m4 in range(4):
                        oc = 128 * m4
                        for dc in range(2):
                            blk = 2 * h + dc
                            last = te.matmul(
                                corrbank[h][:, oc:oc + 128],
                                b_sbf[:, 512 * blk + oc:512 * blk + oc + 128],
                                qcb[pp][:, 128 * blk:128 * (blk + 1)],
                                start=(dc == 0), stop=(dc == 1))
                    c1[(s4, h)] = self.tag("T", last)
                if BP >= 1:
                    for h in range(H):
                        self.w("ve", "T", c1[(s4, h)])
                        self.w("ve", f"q{pp}", lsc[s4])
                        v1[(s4, h)] = self.tag("V", ve.tensor_tensor(
                            ofull[h], corrbank[h][:, :],
                            b_k[:, 2048 * pp + 512 * h:2048 * pp + 512 * (h + 1)],
                            ALU.add))
                self.w("sy", "T", c1[(s4, 3)])
                if BP >= 1:
                    self.w("sy", "V", v1[(s4, 3)])
                sc_loads(s4 + 2)
                if BP < 2:
                    continue
                for h in range(H):
                    self.w("se", "V", v1[(s4, h)])
                    a1[(s4, h)] = self.tag("A", se.activation(
                        b_xb[:, 512 * h:512 * (h + 1)], ofull[h], AF.Square))
                for h in range(H):
                    self.w("te", "A", a1[(s4, h)])
                    for m4 in range(4):
                        last = te.matmul(corrbank[h][0:1, 0:128], ones_col,
                                         b_xb[:, 512 * h + 128 * m4:
                                              512 * h + 128 * (m4 + 1)],
                                         start=(m4 == 0), stop=(m4 == 3))
                    c2[(s4, h)] = self.tag("T", last)
                for h in range(H):
                    row = f_c1[0:1, 1024 + 128 * (h % 2):1152 + 128 * (h % 2)]
                    self.w("se", "T", c2[(s4, h)])
                    self.ack(se.activation(row, corrbank[h][0:1, 0:128], AF.Ln,
                                           scale=cv1(1.0 / 512.0), bias=cv1(1e-5)))
                    a2[(s4, h)] = self.tag("A", se.activation(
                        b_e[0:1, 128 * h:128 * (h + 1)], row, AF.Exp,
                        scale=cv1(-0.5)))
                if BP < 3:
                    continue
                for h in range(H):
                    rrb = PS[1] if h % 2 == 0 else PS[4]
                    self.w("te", "A", a2[(s4, h)])
                    self.w("te", "V", v2.get((s4, h - 2), 0) if h >= 2 else
                           v2.get((s4 - 1, 2 + h), 0))
                    for m4 in range(4):
                        last = te.matmul(rrb[:, 128 * m4:128 * (m4 + 1)],
                                         b_rw[:, 128 * m4:128 * (m4 + 1)],
                                         b_e[0:1, 128 * h:128 * (h + 1)],
                                         start=True, stop=True)
                    c3[(s4, h)] = self.tag("T", last)
                    self.w("ve", "T", c3[(s4, h)])
                    self.w("ve", "V", v1[(s4, h)])
                    self.vck(ve.tensor_tensor(ofull[h], ofull[h], rrb[:, :],
                                              ALU.mult))
                    v2[(s4, h)] = self.tag("V", ve.tensor_tensor(
                        b_st[:, 2048 + 512 * h:2048 + 512 * (h + 1)].rearrange(
                            "p (a b) -> p a b", b=128),
                        ofull[h].rearrange("p (a b) -> p a b", b=128),
                        bass.AP(b_v, (4 * h) * 512 + 128 * s4,
                                [[8192, 128], [512, 4], [1, 128]]), ALU.mult))
                if BP < 4:
                    continue
                btag = {}
                for jc in range(8):
                    psb = PS[7] if jc < 4 else PS[0]
                    oc = 128 * (jc % 4)
                    self.w("te", "V", v2[(s4, 3)])
                    self.w("te", "V", vstt.get((s4 - 1, 3 if jc < 4 else 7), 0))
                    first = True
                    for h in range(H):
                        for m4 in range(4):
                            ec = 4 * h + m4
                            last = te.matmul(
                                psb[:, oc:oc + 128],
                                wa[:, WO + ec * 1024 + 128 * jc:
                                   WO + ec * 1024 + 128 * (jc + 1)],
                                b_st[:, 2048 + 512 * h + 128 * m4:
                                     2048 + 512 * h + 128 * (m4 + 1)],
                                start=first, stop=(ec == 15))
                            first = False
                    if jc in (3, 7):
                        btag[jc] = self.tag("T", last)
                for jc in range(8):
                    psb = PS[7] if jc < 4 else PS[0]
                    oc = 128 * (jc % 4)
                    self.w("ve", "T", btag[3 if jc < 4 else 7])
                    self.w("ve", "q2", lxh[tt])
                    vstt[(s4, jc)] = self.tag("V", ve.scalar_tensor_tensor(
                        f_P[:, jc * 512 + 128 * s4:jc * 512 + 128 * (s4 + 1)],
                        psb[:, oc:oc + 128], f_m[:, M_GA + jc:M_GA + jc + 1],
                        f_x[:, jc * 512 + 128 * s4:jc * 512 + 128 * (s4 + 1)],
                        ALU.mult, ALU.add))
            self.bar()
            self.dma(d_xmid[:, tcols].rearrange("(ic p) c -> p ic c", p=128),
                     f_P[:, :].rearrange("p (ic c) -> p ic c", c=TT))
            self.tap("xmid", f_P[:, 0:1024])
            ve.tensor_copy(b_xb[:, :], f_P[:, :])
            self.bar()
            ln_block(b_xb, b_h1, M_SB, M_CB1, b_h1)
            self.tap("h2", b_h1[:, 0:1024])
            for ic in range(8):
                ve.tensor_reduce(f_c1[:, 1024 + ic:1024 + ic + 1],
                                 b_h1[:, 512 * ic:512 * (ic + 1)], AX.X, ALU.max,
                                 apply_absolute_value=True)
            self.bar()
            ve.tensor_reduce(f_c1[:, 1032:1033], f_c1[:, 1024:1032], AX.X, ALU.max)
            self.bar()
            ve.tensor_tensor(f_m[:, M_RM0:M_RM0 + 1], f_m[:, M_RM0:M_RM0 + 1],
                             f_c1[:, 1032:1033], ALU.max)
            self.bar()
            self.dma(d_h2[:, tcols].rearrange("(ic p) c -> p ic c", p=128),
                     b_h1[:, :].rearrange("p (ic c) -> p ic c", c=TT))
            self.fl()
        if PHASES < 5:
            return self.finish()

        def fold_max(cols, ar_k):
            arin, arout = ar_io[ar_k]
            for idx, c in enumerate(cols):
                self.dma(d_pmax[:, :], f_m[:, c:c + 1])
                self.fl()
                self.dma(f_c1[0:1, 0:128], bass.AP(d_pmax, 0, [[1, 1], [1, 128]]))
                self.fl()
                ve.tensor_reduce(f_c1[0:1, 128 + idx:129 + idx], f_c1[0:1, 0:128],
                                 AX.X, ALU.max)
                self.bar()
            self.dma(f_c1[0:1, 256:264], di["oneh"][:, :])
            self.fl()
            for idx in range(len(cols)):
                ve.tensor_scalar(f_c1[0:1, 192 + 8 * idx:200 + 8 * idx],
                                 f_c1[0:1, 256:264], f_c1[0:1, 128 + idx:129 + idx],
                                 None, ALU.mult)
            if len(cols) == 1:
                ge.memset(f_c1[0:1, 200:208], 0.0)
            self.bar()
            self.dma(arin[:, :], f_c1[0:1, 192:208])
            self.fl()
            ge.collective_compute("AllReduce", ALU.add,
                                  replica_groups=[list(range(8))],
                                  ins=[arin[:, :]], outs=[arout[:, :]]).then_inc(ccsem)
            cc[0] += 1
            ge.wait_ge(ccsem, cc[0])
            self.bar()
            self.dma(f_c1[0:1, 0:16], arout[:, :])
            self.fl()
            for idx in range(len(cols)):
                ve.tensor_reduce(f_c1[0:1, 16 + idx:17 + idx],
                                 f_c1[0:1, 8 * idx:8 * idx + 8], AX.X, ALU.max)
            self.bar()
            return [f_c1[0:1, 16 + idx:17 + idx] for idx in range(len(cols))]

        def qscale(maxap, scol, slot):
            ve.tensor_scalar(f_c1[0:1, 32:33], maxap, 1.0 / QMAX, 1e-12, ALU.mult,
                             ALU.max)
            self.bar()
            ve.reciprocal(f_c1[0:1, 33:34], f_c1[0:1, 32:33])
            self.bar()
            self.dma(d_scal[0:1, slot:slot + 2], f_c1[0:1, 32:34])
            self.fl()
            self.dma(f_m[:, scol:scol + 1],
                     bass.AP(d_scal, slot, [[0, 128], [1, 1]]))
            self.dma(f_m[:, scol + 1:scol + 2],
                     bass.AP(d_scal, slot + 1, [[0, 128], [1, 1]]))
            self.fl()

        def bcast_scalar(srcap, dstcol, slot):
            self.dma(d_scal[0:1, slot:slot + 1], srcap)
            self.fl()
            self.dma(f_m[:, dstcol:dstcol + 1],
                     bass.AP(d_scal, slot, [[0, 128], [1, 1]]))
            self.fl()

        mx = fold_max([M_RM0], 0)[0]
        qscale(mx, M_SCAL, 0)
        S0, R0 = M_SCAL, M_SCAL + 1
        self.tap("sc0", f_m[:, 72:80])

        fxi = f_x.bitcast(I16)
        fxu = f_x.bitcast(U16)
        bsti = b_st.bitcast(I16)
        self.bar()

        def dequant(src, dst, n_oc, icols):
            nb = icols // 32
            tiles = [f_c0[:, 0:icols], (f_c1 if icols == 1024 else f_P)[:, 0:icols]]
            ls = {}
            vtile = {}
            vout = {}
            gt_ = {}
            ls[0] = self.dmaq(0, tiles[0], src[0:128, :])
            if n_oc > 1:
                ls[1] = self.dmaq(1, tiles[1], src[128:256, :])
            for oc in range(n_oc):
                p = oc % 2
                tile = tiles[p]
                so = (1024 + p * 128) if icols == 1024 else (p * 192)
                sv = f_c1[:, so:so + nb]
                rvo = so + (64 if icols == 1024 else 96)
                rv = f_c1[:, rvo:rvo + nb]
                self.w("ve", f"q{p}", ls[oc])
                self.vck(ve.tensor_reduce(sv.rearrange("p (a b) -> p a b", b=1),
                                 tile.rearrange("p (a b) -> p a b", b=32), AX.X,
                                 ALU.max, apply_absolute_value=True))
                self.vck(ve.tensor_scalar(sv, sv, 1.0 / 7.0, 1e-12, ALU.mult,
                                          ALU.max))
                self.vck(ve.reciprocal(rv, sv))
                ki = i_carve[:, 0:icols] if p == 0 else fxi[:, 0:icols]
                vtile[oc] = self.vck(ve.tensor_tensor(
                    ki.rearrange("p (a b) -> p a b", b=32),
                    tile.rearrange("p (a b) -> p a b", b=32),
                    bass.AP(f_c1, rvo, [[1280, 128], [1, nb], [0, 32]]), ALU.mult))
                if oc + 2 < n_oc:
                    self.w("sy", "V", vtile[oc])
                    ls[oc + 2] = self.dmaq(p, tiles[p],
                                           src[128 * (oc + 2):128 * (oc + 3), :])
                stage = b_st[:, 0:icols] if p == 0 else b_xb[:, 0:icols]
                self.w("ve", f"q{2 + p}", gt_.get(oc - 2, 0))
                vout[oc] = self.tag("V", ve.tensor_tensor(
                    stage.rearrange("p (a b) -> p a b", b=32),
                    ki.rearrange("p (a b) -> p a b", b=32),
                    bass.AP(f_c1, so, [[1280, 128], [1, nb], [0, 32]]), ALU.mult))
                self.w("sy", "V", vout[oc])
                gt_[oc] = self.dmaq(2 + p, dst[128 * oc:128 * (oc + 1), :], stage,
                                    queue="ge")
            self.fl()
        dequant(di["wup"], d_wdq[0], 24, 1024)
        self.tap("dwq", d_wdq[0][0:128, :])
        dequant(di["wgate"], d_wdq[1], 24, 1024)
        dequant(di["wdown"], d_wdqd, 8, 3072)
        self.tap("wdq", b_st[:, 0:1024])

        ge.memset(f_m[:, M_RM0:M_RM0 + 2], 0.0)
        self.bar()
        for gi, (d_raw, runc) in enumerate([(d_up, M_RM0), (d_gate, M_RM1)]):
            for ic in range(8):
                self.dma(wa[:, 3072 * ic:3072 * (ic + 1)],
                         d_wdq[gi][:, 128 * ic:128 * (ic + 1)], transpose=True)
            self.fl()
            if gi == 0:
                self.tap("wat", wa[:, 0:1024])
            lt = {}
            vq = {}
            ttag = {}
            atag = {}
            vred = {}
            gtag = {}
            ptag = {}
            h2bufs = [b_k, b_q]
            h2qbufs = [b_xb, b_h1]
            for tt in range(NT):
                tcols = slice(TT * tt, TT * (tt + 1))
                p = tt % 2
                self.w("sy", "V", ptag.get(tt - 2, 0))
                lt[tt] = self.dmaq(
                    p, h2bufs[p][:, :].rearrange("p (ic c) -> p ic c", c=TT),
                    d_h2[:, tcols].rearrange("(ic p) c -> p ic c", p=128))
                k0 = i_carve[:, 4096 * p:4096 * (p + 1)]
                self.w("ve", f"q{p}", lt[tt])
                ptag[tt] = self.vck(ve.tensor_scalar(
                    k0, h2bufs[p][:, :], f_m[:, R0:R0 + 1], None, ALU.mult))
                self.mc["P"] += 0
                self.w("ve", "T", ttag.get((tt - 2, 23), 0))
                vq[tt] = self.tag("V", ve.tensor_scalar(
                    h2qbufs[p][:, :], k0, f_m[:, S0:S0 + 1], None, ALU.mult))
                for i in range(24):
                    self.w("te", "V", vq[tt])
                    g4 = i // 4
                    self.w("te", "A", atag.get((tt, i - 2), 0) if i >= 2 else
                           atag.get((tt - 1, 22 + i), 0))
                    bank = PS[i % 2]
                    for ic in range(8):
                        last = te.matmul(
                            bank[:, :],
                            wa[:, 3072 * ic + 128 * i:3072 * ic + 128 * (i + 1)],
                            h2qbufs[p][:, 512 * ic:512 * (ic + 1)],
                            start=(ic == 0), stop=(ic == 7))
                    ttag[(tt, i)] = self.tag("T", last)
                    stg = b_st[:, 2048 * (g4 % 2) + 512 * (i % 4):
                               2048 * (g4 % 2) + 512 * (i % 4) + 512]
                    self.w("se", "T", ttag[(tt, i)])
                    self.w("se", f"q{2 + g4 % 2}",
                           gtag.get((tt, g4 - 2), 0) if g4 >= 2 else
                           gtag.get((tt - 1, 4 + g4), 0))
                    self.w("se", "V", vred.get((tt, 4 * (g4 - 2) + 3), 0) if g4 >= 2
                           else vred.get((tt - 1, 4 * (4 + g4) + 3), 0))
                    atag[(tt, i)] = self.tag("A", se.copy(stg, bank[:, :]))
                    self.w("ve", "A", atag[(tt, i)])
                    vred[(tt, i)] = self.tag("V", ve.tensor_reduce(
                        f_c1[:, 32 + i:33 + i], stg, AX.X, ALU.max,
                        apply_absolute_value=True))
                    if gi == 0 and tt == 1 and i == 3:
                        self.tap("stg1", b_st[:, 0:2048])
                    if i % 4 == 3:
                        oc0 = g4 * 4
                        self.w("sy", "A", atag[(tt, i)])
                        for o4 in range(4):
                            gtag[(tt, g4)] = self.dmaq(
                                2 + g4 % 2,
                                d_raw[oc0 + o4][:, tcols],
                                b_st[:, 2048 * (g4 % 2) + 512 * o4:
                                     2048 * (g4 % 2) + 512 * (o4 + 1)], queue="ge")
                self.w("ve", "V", vred[(tt, 23)])
                self.vck(ve.tensor_reduce(f_c1[:, 0:1], f_c1[:, 32:56], AX.X,
                                          ALU.max))
                self.vck(ve.tensor_tensor(f_m[:, runc:runc + 1],
                                          f_m[:, runc:runc + 1], f_c1[:, 0:1],
                                          ALU.max))
            self.fl()
        self.tap("dup", d_up[0][:, 0:1024])
        self.tap("dgt", d_gate[0][:, 0:1024])
        self.tap("dup7", d_up[23][:, T - 1024:T])
        if PHASES < 6:
            return self.finish()

        mup, mgt = fold_max([M_RM0, M_RM1], 1)
        qscale(mup, M_SCAL + 2, 2)
        qscale(mgt, M_SCAL + 4, 4)
        S1U, R1U, S1G, R1G = M_SCAL + 2, M_SCAL + 3, M_SCAL + 4, M_SCAL + 5
        self.tap("sc1", f_m[:, 72:80])

        ge.memset(f_m[:, M_RM0:M_RM0 + 1], 0.0)
        ve.tensor_scalar(f_c1[0:1, 34:35], f_m[0:1, S1G:S1G + 1], 1.0 / 65536.0, None,
                         ALU.mult)
        self.bar()
        bcast_scalar(f_c1[0:1, 34:35], 80, 6)
        C1G = 80
        ls = {}
        ptagk = {}
        ptagy = {}
        vt2 = {}
        gtag = {}
        inbufs = [b_k, b_q]
        sbufs = [f_c0, f_P]
        stages = [b_v[:, 0:4096], b_v[:, 4096:8192], b_st[:, 0:4096],
                  b_xb[:, 0:4096]]

        def p2_spill(oc):
            if oc < 0:
                return
            self.w("sy", "V", vt2[oc])
            gtag[oc] = self.dmaq(2 + oc % 2, d_t2[oc][:, :], stages[oc % 4],
                                 queue="ge")
        for oc in range(24):
            p = oc % 2
            self.w("sy", "V", ptagk.get(oc - 2, 0))
            ls[oc] = self.dmaq(p, inbufs[p][:, :], d_gate[oc][:, :])
            p2_spill(oc - 2)
            kg = i_carve[:, 4096 * p:4096 * (p + 1)]
            yu = fxu[:, 4096 * p:4096 * (p + 1)]
            self.w("ve", f"q{p}", ls[oc])
            ptagk[oc] = self.vck(ve.tensor_scalar(
                kg, inbufs[p][:, :], f_m[:, R1G:R1G + 1], None, ALU.mult))
            self.w("se", "V", ptagk[oc])
            self.w("se", "A", ptagy.get(oc - 2, 0))
            self.ack(se.activation(
                sbufs[p][:, :], kg, AF.Sigmoid, scale=f_m[:, S1G:S1G + 1]))
            ptagy[oc] = self.tag("A", se.activation(
                yu, sbufs[p][:, :], AF.Identity, scale=cv(65536.0)))
            stage = stages[oc % 4]
            self.w("ve", "A", ptagy[oc])
            self.w("ve", f"q{2 + oc % 2}", gtag.get(oc - 4, 0))
            self.vck(ve.scalar_tensor_tensor(stage, kg, f_m[:, C1G:C1G + 1], yu,
                                             ALU.mult, ALU.mult))
            vt2[oc] = self.tag("V", ve.tensor_reduce(
                f_c1[:, 32 + oc:33 + oc], stage, AX.X, ALU.max,
                apply_absolute_value=True))
        p2_spill(22)
        p2_spill(23)
        self.w("ve", "V", vt2[23])
        self.vck(ve.tensor_reduce(f_m[:, M_RM0:M_RM0 + 1], f_c1[:, 32:56], AX.X,
                                  ALU.max))
        self.fl()
        self.tap("t2", b_v[:, 0:1024])
        if PHASES < 7:
            return self.finish()

        mt2 = fold_max([M_RM0], 2)[0]
        qscale(mt2, 81, 7)
        S2, R2 = 81, 82
        ve.tensor_tensor(f_c1[0:1, 34:35], f_m[0:1, S2:S2 + 1],
                         f_m[0:1, S1U:S1U + 1], ALU.mult)
        self.bar()
        bcast_scalar(f_c1[0:1, 34:35], 83, 9)
        C2 = 83

        ge.memset(f_m[:, M_RM0:M_RM0 + 1], 0.0)
        self.bar()
        for oc in range(24):
            self.dma(wa[:, 1024 * oc:1024 * (oc + 1)],
                     d_wdqd[:, 128 * oc:128 * (oc + 1)], transpose=True)
        ls = {}
        ls2 = {}
        ptag = {}
        vop = {}
        gtag = {}
        t2bufs = [b_k, b_h1]
        upbufs = [b_q, b_xb]
        fpb = f_P.bitcast(BF16)
        stages = [b_v[:, 0:4096], b_v[:, 4096:8192], b_st[:, 0:4096],
                  fpb[:, 0:4096]]

        def p3_spill(oc):
            if oc < 0:
                return
            self.w("sy", "V", vop[oc])
            gtag[oc] = self.dmaq(2 + oc % 2, d_opre[oc][:, :], stages[oc % 4],
                                 queue="ge")
        for oc in range(24):
            p = oc % 2
            self.w("sy", "A", ptag.get(oc - 2, 0))
            ls[oc] = self.dmaq(p, t2bufs[p][:, :], d_t2[oc][:, :])
            ls2[oc] = self.dmaq(4 + p, upbufs[p][:, :], d_up[oc][:, :])
            p3_spill(oc - 2)
            k3 = i_carve[:, 4096 * p:4096 * (p + 1)]
            k2 = fxi[:, 4096 * p:4096 * (p + 1)]
            self.w("se", f"q{p}", ls[oc])
            self.w("se", f"q{4 + p}", ls2[oc])
            se.activation(k3, t2bufs[p][:, :], AF.Identity, scale=f_m[:, R2:R2 + 1])
            ptag[oc] = self.tag("A", se.activation(
                k2, upbufs[p][:, :], AF.Identity, scale=f_m[:, R1U:R1U + 1]))
            stage = stages[oc % 4]
            self.w("ve", "A", ptag[oc])
            self.w("ve", f"q{2 + oc % 2}", gtag.get(oc - 4, 0))
            self.vck(ve.scalar_tensor_tensor(stage, k3, f_m[:, C2:C2 + 1], k2,
                                             ALU.mult, ALU.mult))
            vop[oc] = self.tag("V", ve.tensor_reduce(
                f_c1[:, 32 + oc:33 + oc], stage, AX.X, ALU.max,
                apply_absolute_value=True))
        p3_spill(22)
        p3_spill(23)
        self.w("ve", "V", vop[23])
        self.vck(ve.tensor_reduce(f_m[:, M_RM0:M_RM0 + 1], f_c1[:, 32:56], AX.X,
                                  ALU.max))
        self.fl()

        if PHASES < 8:
            return self.finish()

        mop = fold_max([M_RM0], 3)[0]
        qscale(mop, 84, 10)
        S3, R3 = 84, 85

        self.fl()
        oqbufs = [b_k, b_q, b_h1]
        oqcarv = [i_carve[:, 0:4096], i_carve[:, 4096:8192],
                  bsti[:, 0:4096]]

        def oq_ap(oc):
            return oqbufs[oc // 8][:, 512 * (oc % 8):512 * (oc % 8) + 512]
        lx = {}
        lo = {}
        ptag = {}
        voq = {}
        ttag = {}
        vout = {}
        stout = {}
        for tt in range(NT):
            tcols = slice(TT * tt, TT * (tt + 1))
            self.w("sy", "T", ttag.get((tt - 1, 7), 0))
            self.w("sy", "V", vout.get((tt - 1, 7), 0))
            pq = tt % 2
            lx[tt] = self.dmaq(pq, f_x[:, :].rearrange("p (ic c) -> p ic c", c=TT),
                               d_xmid[:, tcols].rearrange("(ic p) c -> p ic c", p=128))
            for g3 in range(3):
                lo[(tt, g3)] = self.dmaq(
                    pq, oqbufs[g3][:, :].rearrange("p (o c) -> p o c", c=512),
                    d_opre[8 * g3:8 * (g3 + 1)][:, :, tcols].rearrange(
                        "o p c -> p o c"))
            for g3 in range(3):
                self.w("se", f"q{pq}", lo[(tt, 2)])
                ptag[(tt, g3)] = self.tag("A", se.activation(
                    oqcarv[g3], oqbufs[g3][:, :], AF.Identity,
                    scale=f_m[:, R3:R3 + 1]))
                self.w("ve", "A", ptag[(tt, g3)])
                voq[(tt, g3)] = self.tag("V", ve.tensor_scalar(
                    oqbufs[g3][:, :], oqcarv[g3], f_m[:, S3:S3 + 1], None, ALU.mult))
            for jc in range(8):
                bank = PS[jc % 2]
                for g3 in range(3):
                    self.w("te", "V", voq[(tt, g3)])
                self.w("te", "V", vout.get((tt, jc - 2), 0) if jc >= 2 else
                       vout.get((tt - 1, 6 + jc), 0))
                for oc in range(24):
                    last = te.matmul(
                        bank[:, :],
                        wa[:, 1024 * oc + 128 * jc:1024 * oc + 128 * (jc + 1)],
                        oq_ap(oc), start=(oc == 0), stop=(oc == 23))
                ttag[(tt, jc)] = self.tag("T", last)
                stg = f_c0[:, 512 * (jc % 2):512 * (jc % 2) + 512]
                self.w("ve", "T", ttag[(tt, jc)])
                self.w("ve", f"q{pq}", lo[(tt, 2)])
                self.w("ve", f"q{6 + jc % 2}",
                       stout.get((tt, jc - 2), 0) if jc >= 2 else
                       stout.get((tt - 1, 6 + jc), 0))
                vout[(tt, jc)] = self.tag("V", ve.scalar_tensor_tensor(
                    stg, bank[:, :], f_m[:, M_GB + jc:M_GB + jc + 1],
                    f_x[:, 512 * jc:512 * (jc + 1)], ALU.mult, ALU.add))
                self.w("sy", "V", vout[(tt, jc)])
                stout[(tt, jc)] = self.dmaq(6 + jc % 2,
                                            outT[128 * jc:128 * (jc + 1), tcols], stg)
        return self.finish()

    def finish(self):
        self.fl()
        return self.nc


def build(taps=()):
    b = Bld(taps)
    with b.ctx:
        nc = b.build()
    return nc


def kernel(**inputs):
    x = np.asarray(inputs["L0_attn_hidden_states"], np.float32)
    c = np.asarray(inputs["L0_c"], np.float32)
    bgk2 = np.asarray(inputs["L0_attn_bgk2"], np.float32)
    rmsw = np.asarray(inputs["L0_attn_rms_w"], np.float32)

    base = {
        "Wq": np.ascontiguousarray(inputs["L0_attn_Wq"], np.float32),
        "Wk": np.ascontiguousarray(inputs["L0_attn_Wk"], np.float32),
        "Wv": np.ascontiguousarray(inputs["L0_attn_Wv"], np.float32),
        "Wg": np.ascontiguousarray(inputs["L0_attn_Wg"], np.float32),
        "Wgk1": np.ascontiguousarray(inputs["L0_attn_Wgk1"], np.float32),
        "Wgk2": np.ascontiguousarray(inputs["L0_attn_Wgk2"], np.float32),
        "Wo": np.ascontiguousarray(inputs["L0_attn_Wo"], np.float32),
        "wup": np.ascontiguousarray(inputs["L0_ffn_w_up"], np.float32),
        "wgate": np.ascontiguousarray(inputs["L0_ffn_w_gate"], np.float32),
        "wdown": np.ascontiguousarray(inputs["L0_ffn_w_down"], np.float32),
        "wmod": np.ascontiguousarray(inputs["L0_wmod"], np.float32),
        "bgk2f": np.ascontiguousarray(bgk2.reshape(8, 128).T),
        "rmsw": np.ascontiguousarray(rmsw.reshape(1, 512)),
    }
    nc = build(DEBUG_TAPS)
    in_maps = []
    for core in range(8):
        b_, s_ = divmod(core, 2)
        m = dict(base)
        m["xT"] = np.ascontiguousarray(x[b_, 4096 * s_:4096 * (s_ + 1), :].T)
        m["cfm"] = np.ascontiguousarray(c[b_].reshape(8, 128).T)
        sel = np.zeros((1, 8), np.float32)
        if s_ == 1:
            sel[0, core - 1] = 1.0
        m["selS"] = sel
        oh = np.zeros((1, 8), np.float32)
        oh[0, core] = 1.0
        m["oneh"] = oh
        in_maps.append(m)
    res = run_bass_kernel_spmd(nc, in_maps, core_ids=list(range(8)))
    out = np.empty((4, 8192, 1024), np.float32)
    for core in range(8):
        b_, s_ = divmod(core, 2)
        out[b_, 4096 * s_:4096 * (s_ + 1), :] = res.results[core]["outT"].T
    kernel._results = res.results
    return out


# version 15
# speedup vs baseline: 1.1459x; 1.0626x over previous
import numpy as np
import ml_dtypes
from contextlib import ExitStack

import concourse.bass as bass
import concourse.mybir as mybir
from concourse.bass_utils import run_bass_kernel_spmd

F32 = mybir.dt.float32
BF16 = mybir.dt.bfloat16
I16 = mybir.dt.int16
U16 = mybir.dt.uint16
U8 = mybir.dt.uint8
AF = mybir.ActivationFunctionType
ALU = mybir.AluOpType
AX = mybir.AxisListType

D = 1024
T = 4096
TT = 512
NT = T // TT
H = 4
DV = 2048
SC = 128
NSC = T // SC
FF = 3072
LN16 = float(np.log(16.0))
QMAX = 32767.0

DEBUG_TAPS = set()
PHASES = 99
BP = 99


class Bld:
    def __init__(self, taps=()):
        self.nc = bass.Bass("TRN2", target_bir_lowering=False, dynamic_dma_scratch_size=8192)
        self.ctx = ExitStack()
        self.taps = set(taps)
        self.tapn = {}
        self.dctr = 0

    def sb(self, name, shape, dt):
        return self.ctx.enter_context(self.nc.sbuf_tensor(name, shape, dt))

    def ps(self, name, shape, dt=F32):
        return self.ctx.enter_context(self.nc.psum_tensor(name, shape, dt))

    ENGS = ("te", "ve", "se", "ge", "sy")

    def bar(self):
        self.nc.all_engine_barrier()
        for e in self.ENGS:
            for k in self.mc:
                self.lastw[(e, k)] = self.mc[k]

    def eng(self, name):
        return {"te": self.nc.tensor, "ve": self.nc.vector, "se": self.nc.scalar,
                "ge": self.nc.gpsimd, "sy": self.nc.sync}[name]

    def tag(self, key, inst):
        inst.then_inc(self.sems[key], 1)
        self.mc[key] += 1
        return self.mc[key]

    def w(self, ename, key, val):
        if val is None or val <= 0:
            return
        if self.lastw.get((ename, key), 0) >= val:
            return
        self.eng(ename).wait_ge(self.sems[key], val)
        self.lastw[(ename, key)] = val

    def dma(self, out, in_, transpose=False):
        self.nc.sync.dma_start(out=out, in_=in_, transpose=transpose).then_inc(
            self.dsem, 16)
        self.dctr += 16
        self.mc["S"] = self.dctr
        return self.dctr

    def dmag(self, out, in_):
        self.nc.gpsimd.dma_start(out=out, in_=in_).then_inc(self.gsem, 16)
        self.gctr += 16
        self.mc["G"] = self.gctr
        return self.gctr

    def vck(self, inst):
        v = self.tag("V", inst)
        self.nc.vector.wait_ge(self.sems["V"], v)
        self.lastw[("ve", "V")] = v
        return v

    def ack(self, inst):
        v = self.tag("A", inst)
        self.nc.scalar.wait_ge(self.sems["A"], v)
        self.lastw[("se", "A")] = v
        return v

    def dmaq(self, qi, out, in_, transpose=False, queue="sy"):
        eng = self.nc.sync
        k = f"q{qi}"
        eng.dma_start(out=out, in_=in_, transpose=transpose).then_inc(self.sems[k], 16)
        self.mc[k] += 16
        return self.mc[k]

    def fl(self):
        self.nc.sync.wait_ge(self.dsem, self.dctr)
        for i in range(8):
            k = f"q{i}"
            if self.mc[k]:
                self.nc.sync.wait_ge(self.sems[k], self.mc[k])
        if self.gctr:
            self.nc.gpsimd.wait_ge(self.gsem, self.gctr)
        self.bar()

    def tap(self, name, ap):
        if name not in self.taps:
            return
        k = self.tapn.get(name, 0)
        self.tapn[name] = k + 1
        t = self.nc.dram_tensor(f"tap_{name}_{k}", list(ap.shape), ap.dtype,
                                kind="ExternalOutput")
        idx = tuple(slice(0, s) for s in ap.shape)
        self.dma(t[idx], ap)
        self.fl()

    def pingpong(self, n, emit_mms, emit_evac, banks):
        nc = self.nc
        nb = len(banks)
        pe0 = self.ppe_ctr
        ev0 = self.pev_ctr
        for i in range(n):
            if i >= nb:
                nc.tensor.wait_ge(self.sev, ev0 + i - nb + 1)
            emit_mms(i, banks[i % nb]).then_inc(self.spe, 1)
            self.ppe_ctr += 1
        for i in range(n):
            nc.scalar.wait_ge(self.spe, pe0 + i + 1)
            emit_evac(i, banks[i % nb]).then_inc(self.sev, 1)
            self.pev_ctr += 1
        self.bar()

    def build(self):
        nc = self.nc
        di = {}
        for name, shape in [
            ("xT", [D, T]), ("Wq", [D, D]), ("Wk", [D, D]), ("Wv", [D, DV]),
            ("Wg", [D, DV]), ("Wgk1", [D, 16]), ("Wgk2", [16, D]),
            ("bgk2f", [128, 8]), ("rmsw", [1, 512]), ("Wo", [DV, D]),
            ("wup", [FF, D]), ("wgate", [FF, D]), ("wdown", [D, FF]),
            ("wmod", [D, 6 * D]), ("cfm", [128, 8]), ("selS", [1, 8]),
            ("oneh", [1, 8]),
        ]:
            di[name] = nc.dram_tensor(name, shape, F32, kind="ExternalInput")
        outT = nc.dram_tensor("outT", [D, T], F32, kind="ExternalOutput")

        d_h1 = nc.dram_tensor("d_h1", [D, T], BF16)
        d_h2 = nc.dram_tensor("d_h2", [D, T], BF16)
        d_xmid = nc.dram_tensor("d_xmid", [D, T], F32)
        d_oloc = nc.dram_tensor("d_oloc", [NSC, H, 128, 512], BF16)
        d_qc = nc.dram_tensor("d_qc", [NSC, D, 128], BF16)
        d_kh = nc.dram_tensor("d_kh", [2, D, 128], BF16)
        d_sin = nc.dram_tensor("d_sin", [D, 512], BF16)
        d_sout = nc.dram_tensor("d_sout", [8 * D, 512], BF16)
        d_up = nc.dram_tensor("d_up", [24, 128, T], BF16)
        d_gate = nc.dram_tensor("d_gate", [24, 128, T], BF16)
        d_t2 = nc.dram_tensor("d_t2", [24, 128, T], BF16)
        d_opre = nc.dram_tensor("d_opre", [24, 128, T], BF16)
        d_wdq = nc.dram_tensor("d_wdq", [2, FF, D], BF16)
        d_wdqd = nc.dram_tensor("d_wdqd", [D, FF], BF16)
        d_scal = nc.dram_tensor("d_scal", [1, 64], F32)
        d_pmax = nc.dram_tensor("d_pmax", [128, 1], F32)
        ar_io = [(nc.dram_tensor(f"d_arin{k}", [1, 16], F32),
                  nc.dram_tensor(f"d_arout{k}", [1, 16], F32)) for k in range(4)]
        mask_np = np.triu(np.ones((128, 128), np.float32)).astype(ml_dtypes.bfloat16)
        d_mask = nc.inline_tensor(mask_np, "d_mask")

        WAC = 32896
        wa = self.sb("wa", [128, WAC], BF16)
        f_x = self.sb("f_x", [128, 4096], F32)
        f_P = self.sb("f_P", [128, 4096], F32)
        f_S = self.sb("f_S", [128, 4096], F32)
        f_c0 = self.sb("f_c0", [128, 4096], F32)
        f_c1 = self.sb("f_c1", [128, 1280], F32)
        f_m = self.sb("f_m", [128, 96], F32)
        cp = self.sb("cp", [128, 10], F32)
        cm1 = self.sb("cm1", [128, 1], F32)
        b_q = self.sb("b_q", [128, 4096], BF16)
        b_k = self.sb("b_k", [128, 4096], BF16)
        b_v = self.sb("b_v", [128, 8192], BF16)
        b_h1 = self.sb("b_h1", [128, 4096], BF16)
        b_xb = self.sb("b_xb", [128, 4096], BF16)
        b_e = self.sb("b_e", [128, 2048], BF16)
        b_sbf = self.sb("b_sbf", [128, 4096], BF16)
        b_st = self.sb("b_st", [128, 4096], BF16)
        b_kh = self.sb("b_kh", [128, 1024], BF16)
        b_ms = self.sb("b_ms", [128, 128], BF16)
        b_rw = self.sb("b_rw", [1, 512], BF16)
        b_on = self.sb("b_on", [128, 1], BF16)
        b_or = self.sb("b_or", [1, 128], BF16)
        wgk2 = self.sb("wgk2", [16, 1024], BF16)
        bgk2 = self.sb("bgk2", [128, 16], F32)
        u8m = self.sb("u8m", [128, 8], U8)
        i_carve = f_S.bitcast(I16)
        u_carve = f_S.bitcast(U16)
        PS = [self.ps(f"PS{i}", [128, 512]) for i in range(8)]

        self.dsem = self.ctx.enter_context(nc.semaphore("dS"))
        self.gsem = self.ctx.enter_context(nc.semaphore("dG"))
        self.spe = self.ctx.enter_context(nc.semaphore("sPE"))
        self.sev = self.ctx.enter_context(nc.semaphore("sEV"))
        ccsem = self.ctx.enter_context(nc.semaphore("sCC"))
        self.sems = {k: self.ctx.enter_context(nc.semaphore("m" + k))
                     for k in ("T", "V", "A", "P")}
        self.sems["S"] = self.dsem
        self.sems["G"] = self.gsem
        for i in range(8):
            self.sems[f"q{i}"] = self.ctx.enter_context(nc.semaphore(f"q{i}"))
        self.mc = {"T": 0, "V": 0, "A": 0, "P": 0, "S": 0, "G": 0}
        for i in range(8):
            self.mc[f"q{i}"] = 0
        self.lastw = {}
        self.gctr = 0
        self.ppe_ctr = 0
        self.pev_ctr = 0
        cc = [0]

        te, ve, se, ge = nc.tensor, nc.vector, nc.scalar, nc.gpsimd

        CONSTS = [-LN16, -1.0 / 16.0, 1.0 / 16.0, -0.5, 1e-5, 1.0 / 1024.0,
                  1.0 / 512.0, 1.0, 65536.0]
        for j, v in enumerate(CONSTS):
            ge.memset(cp[:, j:j + 1], float(v))
        ge.memset(cm1[:, :], -1.0)

        def cv(v):
            return cp[:, CONSTS.index(v):CONSTS.index(v) + 1]

        def cv1(v):
            return cp[0:1, CONSTS.index(v):CONSTS.index(v) + 1]

        M_SA, M_CA1, M_GA, M_SB, M_CB1, M_GB = 0, 8, 16, 24, 32, 40
        M_EXPG, M_EXPL, M_LAM = 48, 56, 64
        M_RM0, M_RM1 = 72, 73
        M_SCAL = 74

        ge.memset(f_m[:, M_LAM:M_LAM + 8], 0.0)
        ge.memset(f_m[:, M_RM0:M_RM0 + 2], 0.0)
        ge.memset(b_on[:, :], 1.0)
        ge.memset(b_or[:, :], 1.0)
        ones_col = b_on[:, 0:1]
        ones_row = b_or[0:1, :]

        self.dma(b_ms[:, :], d_mask[:, :])
        self.dma(f_m[:, M_SA:M_SA + 8], di["cfm"][:, :])
        self.dma(f_c1[0:1, 0:512], di["rmsw"][:, :])
        self.fl()
        ve.tensor_copy(b_rw[:, :], f_c1[0:1, 0:512])
        se.activation(f_m[:, M_SA:M_SA + 8], f_m[:, M_SA:M_SA + 8], AF.Silu)
        self.bar()

        siluc = f_m[:, M_SA:M_SA + 8]
        ge.memset(f_c0[:, 0:48], 0.0)
        self.bar()
        for ic in range(8):
            for hh in range(2):
                self.dma(f_x[:, 0:3072], di["wmod"][128 * ic:128 * (ic + 1),
                                                    3072 * hh:3072 * (hh + 1)])
                self.fl()
                for j24 in range(24):
                    jc = 24 * hh + j24
                    te.matmul(PS[0][:, jc:jc + 1], f_x[:, 128 * j24:128 * (j24 + 1)],
                              siluc[:, ic:ic + 1], start=True, stop=True)
                self.bar()
            se.copy(f_c1[:, 16:64], PS[0][:, 0:48])
            self.bar()
            ve.tensor_tensor(f_c0[:, 0:48], f_c0[:, 0:48], f_c1[:, 16:64], ALU.add)
            self.bar()
        ve.tensor_copy(f_m[:, 0:48], f_c0[:, 0:48])
        self.bar()
        ve.tensor_scalar(f_m[:, M_CA1:M_CA1 + 8], f_m[:, M_CA1:M_CA1 + 8], 1.0, None,
                         ALU.add)
        ve.tensor_scalar(f_m[:, M_CB1:M_CB1 + 8], f_m[:, M_CB1:M_CB1 + 8], 1.0, None,
                         ALU.add)
        self.bar()
        for off in (M_GA, M_GB):
            g = f_m[:, off:off + 8]
            ve.tensor_scalar(u8m[:, :], g, 0.0, None, ALU.is_ge)
            ve.tensor_scalar(f_c1[:, 0:8], g, 0.25, None, ALU.max)
            ve.tensor_scalar(f_c1[:, 8:16], g, -0.25, None, ALU.min)
            self.bar()
            ve.select(g, u8m[:, :], f_c1[:, 0:8], f_c1[:, 8:16])
            self.bar()
        self.tap("mod", f_m[:, 0:48])
        if PHASES < 1:
            return self.finish()

        WQ, WK, WV, GK1 = 0, 8192, 16384, 32768

        def load_w(dst_off, src, ic_cols, n_ic):
            lw = {}
            cw = {}
            for ic in range(n_ic):
                p = ic % 2
                self.w("sy", "V", cw.get(ic - 2, 0))
                lw[ic] = self.dmaq(6 + p, f_x[:, 2048 * p:2048 * p + ic_cols],
                                   src[128 * ic:128 * (ic + 1), :])
                self.w("ve", f"q{6 + p}", lw[ic])
                cw[ic] = self.tag("V", ve.tensor_copy(
                    wa[:, dst_off + ic * ic_cols:dst_off + (ic + 1) * ic_cols],
                    f_x[:, 2048 * p:2048 * p + ic_cols]))
            self.w("sy", "V", cw[n_ic - 1])
            self.bar()

        load_w(WQ, di["Wq"], 1024, 8)
        load_w(WK, di["Wk"], 1024, 8)
        load_w(WV, di["Wv"], 2048, 8)
        load_w(GK1, di["Wgk1"], 16, 8)
        self.dma(f_c1[0:16, 0:1024], di["Wgk2"][:, :])
        self.fl()
        ve.tensor_copy(wgk2[:, :], f_c1[0:16, 0:1024])
        self.bar()
        self.dma(f_c1[:, 0:8], di["bgk2f"][:, :])
        self.fl()
        ve.tensor_copy(bgk2[:, 0:8], f_c1[:, 0:8])
        ve.tensor_scalar(bgk2[:, 8:16], f_c1[:, 0:8], -1.0, None, ALU.mult)
        self.bar()
        ge.memset(f_S[:, :], 0.0)
        ge.memset(b_sbf[:, :], 0.0)
        self.bar()

        def ln_block(src_b, sq_b, sa_off, ca1_off, out_b):
            se.activation(sq_b[:, :], src_b[:, :], AF.Square)
            self.bar()
            for ic in range(8):
                te.matmul(PS[2][0:1, :], ones_col, src_b[:, 512 * ic:512 * (ic + 1)],
                          start=(ic == 0), stop=(ic == 7))
            for ic in range(8):
                te.matmul(PS[3][0:1, :], ones_col, sq_b[:, 512 * ic:512 * (ic + 1)],
                          start=(ic == 0), stop=(ic == 7))
            self.bar()
            st = f_c1[0:1, :]
            ve.tensor_scalar(st[:, 0:512], PS[2][0:1, :], 1.0 / 1024.0, None, ALU.mult)
            ve.tensor_scalar(st[:, 512:1024], PS[3][0:1, :], 1.0 / 1024.0, None,
                             ALU.mult)
            self.bar()
            ve.tensor_tensor(f_c0[0:1, 0:512], st[:, 0:512], st[:, 0:512], ALU.mult)
            self.bar()
            ve.tensor_tensor(st[:, 512:1024], st[:, 512:1024], f_c0[0:1, 0:512],
                             ALU.subtract)
            self.bar()
            se.activation(st[:, 512:1024], st[:, 512:1024], AF.Ln, bias=cv1(1e-5))
            self.bar()
            se.activation(st[:, 512:1024], st[:, 512:1024], AF.Exp, scale=cv1(-0.5))
            self.bar()
            ve.tensor_tensor(st[:, 0:512], st[:, 0:512], st[:, 512:1024], ALU.mult)
            self.bar()
            rstd_b = b_e[0:1, 0:512]
            mrstd_b = b_e[0:1, 512:1024]
            ve.tensor_copy(rstd_b, st[:, 512:1024])
            ve.tensor_copy(mrstd_b, st[:, 0:512])
            self.bar()
            te.matmul(PS[4][:, :], ones_row, rstd_b, start=True, stop=True)
            te.matmul(PS[5][:, :], ones_row, mrstd_b, start=True, stop=True)
            self.bar()
            for ic in range(8):
                o = (ic % 4) * 1024
                dst = (f_c0 if ic < 4 else f_P)
                ve.tensor_tensor(dst[:, o:o + 512], src_b[:, 512 * ic:512 * (ic + 1)],
                                 PS[4][:, :], ALU.mult)
            self.bar()
            for ic in range(8):
                o = (ic % 4) * 1024
                dst = (f_c0 if ic < 4 else f_P)
                ve.tensor_tensor(dst[:, o:o + 512], dst[:, o:o + 512], PS[5][:, :],
                                 ALU.subtract)
            self.bar()
            for ic in range(8):
                o = (ic % 4) * 1024
                dst = (f_c0 if ic < 4 else f_P)
                se.activation(out_b[:, 512 * ic:512 * (ic + 1)], dst[:, o:o + 512],
                              AF.Identity, bias=f_m[:, sa_off + ic:sa_off + ic + 1],
                              scale=f_m[:, ca1_off + ic:ca1_off + ic + 1])
            self.bar()

        for tt in range(NT):
            tcols = slice(TT * tt, TT * (tt + 1))
            self.dma(f_x[:, :].rearrange("p (ic c) -> p ic c", c=TT),
                     di["xT"][:, tcols].rearrange("(ic p) c -> p ic c", p=128))
            self.fl()
            ve.tensor_copy(b_xb[:, :], f_x[:, :])
            self.bar()
            ln_block(b_xb, b_h1, M_SA, M_CA1, b_h1)
            self.tap("h1", b_h1[:, 0:1024])
            self.dma(d_h1[:, tcols].rearrange("(ic p) c -> p ic c", p=128),
                     b_h1[:, :].rearrange("p (ic c) -> p ic c", c=TT))
            if PHASES < 2:
                self.fl()
                continue

            def qk(w_off, out_sb):
                def mms(i, bank):
                    for ic in range(8):
                        last = te.matmul(
                            bank[:, :],
                            wa[:, w_off + ic * 1024 + 128 * i:
                               w_off + ic * 1024 + 128 * (i + 1)],
                            b_h1[:, 512 * ic:512 * (ic + 1)],
                            start=(ic == 0), stop=(ic == 7))
                    return last

                def evac(i, bank):
                    return se.copy(out_sb[:, 512 * i:512 * (i + 1)], bank[:, :])
                self.pingpong(8, mms, evac, [PS[0], PS[1]])
            qk(WQ, b_q)
            qk(WK, b_k)
            self.tap("q", b_q[:, 0:1024])
            self.tap("k", b_k[:, 0:1024])

            def v_mms(i, bank):
                tc, eg = divmod(i, 4)
                for ic in range(8):
                    last = te.matmul(
                        bank[:, :],
                        b_h1[:, 512 * ic + 128 * tc:512 * ic + 128 * (tc + 1)],
                        wa[:, WV + ic * 2048 + 512 * eg:WV + ic * 2048 + 512 * (eg + 1)],
                        start=(ic == 0), stop=(ic == 7))
                return last

            def v_evac(i, bank):
                tc, eg = divmod(i, 4)
                return se.copy(
                    b_v[:, tc * 2048 + 512 * eg:tc * 2048 + 512 * (eg + 1)], bank[:, :])
            self.pingpong(16, v_mms, v_evac, [PS[0], PS[1]])
            self.tap("v", b_v[:, 0:2048])

            for ic in range(8):
                te.matmul(PS[0][0:16, :], wa[:, GK1 + 16 * ic:GK1 + 16 * (ic + 1)],
                          b_h1[:, 512 * ic:512 * (ic + 1)], start=(ic == 0),
                          stop=(ic == 7))
            self.bar()
            t1b = b_e[0:16, 1024:1536]
            se.copy(t1b, PS[0][0:16, :])
            self.bar()
            for r in range(4):
                for half in range(2):
                    ic = 2 * r + half
                    te.matmul(PS[3 + half][:, :], wgk2[:, 128 * ic:128 * (ic + 1)],
                              t1b, start=True, stop=True)
                self.bar()
                u0 = f_c0[:, 0:1024]
                aa = f_c0[:, 1024:2048]
                bb = f_c0[:, 2048:3072]
                for half in range(2):
                    ic = 2 * r + half
                    se.activation(u0[:, 512 * half:512 * (half + 1)],
                                  PS[3 + half][:, :], AF.Identity,
                                  bias=bgk2[:, ic:ic + 1])
                self.bar()
                se.activation(aa, u0, AF.Abs)
                self.bar()
                se.activation(bb, aa, AF.Exp, scale=cm1[:, 0:1])
                self.bar()
                se.activation(aa, bb, AF.Ln, bias=cv(1.0))
                self.bar()
                ve.tensor_scalar(bb, u0, -1.0, 0.0, ALU.mult, ALU.max)
                self.bar()
                ve.tensor_tensor(f_P[:, 1024 * r:1024 * (r + 1)], aa, bb, ALU.add)
                self.bar()
            self.tap("P", f_P[:, 0:1024])
            if PHASES < 3:
                continue

            scst = getattr(self, "_scst", None)
            if scst is None:
                scst = self._scst = {"pm": 0, "ee": 0, "kq": 0, "mk": 0, "oh": 0,
                                     "evA": 0, "sb1": 0, "stt": 0, "lam": 0,
                                     "kw": {}, "kr": {}, "qout": {}, "osp": {}}
            for s4 in range(4):
                sc = 4 * tt + s4
                co = 128 * s4
                pq = sc % 2
                self.w("ve", "A", scst["ee"])
                for ic in range(8):
                    base = ic * 512 + co
                    last = ve.tensor_scalar(f_P[:, base + 64:base + 128],
                                            f_P[:, base + 64:base + 128],
                                            f_P[:, base + 63:base + 64], None,
                                            ALU.add)
                scst["pm"] = self.tag("V", last)
                pv = f_P[:, :].rearrange("p (ic c) -> p ic c", c=512)[:, :, co:co + 128]
                pg = f_P[:, :].rearrange("p (ic c) -> p ic c",
                                         c=512)[:, :, co + 127:co + 128]
                ep = b_e[:, 0:1024].rearrange("p (ic c) -> p ic c", c=128)
                em = b_e[:, 1024:2048].rearrange("p (ic c) -> p ic c", c=128)
                self.w("se", "V", scst["pm"])
                self.w("se", "V", scst["lam"])
                se.activation(ep, pv, AF.Exp, scale=cv(-1.0 / 16.0), bias=cv(-LN16))
                se.activation(em, pv, AF.Exp, scale=cv(1.0 / 16.0))
                se.activation(f_m[:, M_EXPG:M_EXPG + 8].rearrange(
                    "p (a b) -> p a b", b=1), pg, AF.Exp, scale=cv(-1.0 / 16.0))
                scst["ee"] = self.tag("A", se.activation(
                    f_m[:, M_EXPL:M_EXPL + 8], f_m[:, M_LAM:M_LAM + 8],
                    AF.Exp, scale=cv(-1.0 / 16.0)))
                qv = b_q[:, :].rearrange("p (dk c) -> p dk c", c=512)[:, :, co:co + 128]
                kv = b_k[:, :].rearrange("p (dk c) -> p dk c", c=512)[:, :, co:co + 128]
                self.w("ve", "A", scst["ee"])
                ve.tensor_tensor(qv, qv, ep, ALU.mult)
                self.vck(ve.tensor_tensor(kv, kv, em, ALU.mult))
                self.w("ve", f"q{4 + (1 - pq)}", scst["qout"].get(sc - 1, 0))
                self.w("ve", f"q{4 + pq}", scst["kr"].get(sc - 2, 0))
                ve.tensor_tensor(
                    b_st[:, 0:1024].rearrange("p (dc c) -> p dc c", c=128), kv,
                    bass.AP(f_m, M_EXPG, [[96, 128], [1, 8], [0, 128]]), ALU.mult)
                scst["kq"] = self.tag("V", ve.tensor_tensor(
                    b_st[:, 1024:2048].rearrange("p (dc c) -> p dc c", c=128), qv,
                    bass.AP(f_m, M_EXPL, [[96, 128], [1, 8], [0, 128]]), ALU.mult))
                self.w("sy", "V", scst["kq"])
                kw = self.dmaq(4 + pq, d_kh[pq].rearrange("(dc p) c -> p dc c", p=128),
                               b_st[:, 0:1024].rearrange("p (dc c) -> p dc c", c=128))
                scst["kw"][sc] = kw
                scst["qout"][sc] = self.dmaq(
                    4 + pq, d_qc[sc].rearrange("(dc p) c -> p dc c", p=128),
                    b_st[:, 1024:2048].rearrange("p (dc c) -> p dc c", c=128))
                self.w("sy", f"q{4 + pq}", scst["qout"][sc])
                self.w("sy", "T", scst.get("ut", 0))
                scst["kr"][sc] = self.dmaq(4 + pq, b_kh[:, :], d_kh[pq][:, :],
                                           transpose=True)
                self.w("te", "V", scst["kq"])
                for h in range(H):
                    for dc in range(2):
                        base = (2 * h + dc) * 512 + co
                        last = te.matmul(PS[2][:, 128 * h:128 * (h + 1)],
                                         b_k[:, base:base + 128],
                                         b_q[:, base:base + 128],
                                         start=(dc == 0), stop=(dc == 1))
                at = self.tag("T", last)
                self.w("ve", "T", at)
                scst["mk"] = self.tag("V", ve.tensor_tensor(
                    b_st[:, 2048:2560].rearrange("p (h c) -> p h c", c=128),
                    PS[2][:, :].rearrange("p (h c) -> p h c", c=128),
                    bass.AP(b_ms, 0, [[128, 128], [0, 4], [1, 128]]), ALU.mult))
                ohh = {}
                self.w("te", "V", scst["mk"])
                self.w("te", "A", scst["sb1"])
                for h in range(H):
                    pso = PS[3 + h]
                    for m4 in range(4):
                        oc = 128 * m4
                        for dc in range(2):
                            blk = 2 * h + dc
                            te.matmul(pso[:, oc:oc + 128],
                                      b_sbf[:, 512 * blk + oc:512 * blk + oc + 128],
                                      b_q[:, blk * 512 + co:blk * 512 + co + 128],
                                      start=(dc == 0), stop=False)
                        last = te.matmul(pso[:, oc:oc + 128],
                                         b_v[:, s4 * 2048 + 512 * h + oc:
                                            s4 * 2048 + 512 * h + oc + 128],
                                         b_st[:, 2048 + 128 * h:2048 + 128 * (h + 1)],
                                         start=False, stop=True)
                    ohh[h] = self.tag("T", last)
                scst["oh"] = ohh[3]
                for h in range(H):
                    stg = b_st[:, 2560 + 512 * (h % 2):3072 + 512 * (h % 2)]
                    self.w("se", "T", ohh[h])
                    self.w("se", f"q{6 + h % 2}", scst["osp"].get((sc, h - 2), 0)
                           if h >= 2 else scst["osp"].get((sc - 1, 2 + h), 0))
                    ea = self.tag("A", se.copy(stg, PS[3 + h][:, :]))
                    self.w("sy", "A", ea)
                    scst["osp"][(sc, h)] = self.dmaq(6 + h % 2, d_oloc[sc, h], stg)
                scst["evA"] = ea
                for rnd in range(2):
                    banks = [PS[0], PS[1], PS[7], PS[2]]
                    self.w("te", f"q{4 + pq}", scst["kr"][sc])
                    if rnd == 1:
                        self.w("te", "V", scst["stt"])
                    for j in range(4):
                        blk = 4 * rnd + j
                        h = blk // 2
                        last = te.matmul(banks[j][:, :],
                                         b_kh[:, 128 * blk:128 * (blk + 1)],
                                         b_v[:, s4 * 2048 + 512 * h:
                                            s4 * 2048 + 512 * (h + 1)],
                                         start=True, stop=True)
                    ut = self.tag("T", last)
                    scst["ut"] = ut
                    self.w("ve", "T", ut)
                    for j in range(4):
                        blk = 4 * rnd + j
                        last = ve.scalar_tensor_tensor(
                            f_S[:, 512 * blk:512 * (blk + 1)],
                            f_S[:, 512 * blk:512 * (blk + 1)],
                            f_m[:, M_EXPG + blk:M_EXPG + blk + 1],
                            banks[j][:, :], ALU.mult, ALU.add)
                    scst["stt"] = self.tag("V", last)
                    self.w("se", "V", scst["stt"])
                    for j in range(4):
                        blk = 4 * rnd + j
                        last = se.copy(b_sbf[:, 512 * blk:512 * (blk + 1)],
                                       f_S[:, 512 * blk:512 * (blk + 1)])
                    scst["sb1"] = self.tag("A", last)
                scst["lam"] = self.tag("V", ve.tensor_tensor(
                    f_m[:, M_LAM:M_LAM + 8], f_m[:, M_LAM:M_LAM + 8],
                    pg.rearrange("p a b -> p (a b)"), ALU.add))
            self.bar()
            self.tap("S", f_S[:, 0:1024])
        if PHASES < 4:
            return self.finish()

        self.dma(d_sin[:, :].rearrange("(blk p) c -> p blk c", p=128),
                 b_sbf[:, :].rearrange("p (blk c) -> p blk c", c=512))
        self.fl()
        ge.collective_compute("AllGather", ALU.bypass,
                              replica_groups=[list(range(8))],
                              ins=[d_sin[:, :]], outs=[d_sout[:, :]]).then_inc(ccsem)
        cc[0] += 1
        ge.wait_ge(ccsem, cc[0])
        self.bar()
        self.dma(f_c1[:, 0:8], bass.AP(di["selS"], 0, [[0, 128], [1, 8]]))
        self.fl()
        ge.memset(f_S[:, :], 0.0)
        self.bar()
        for r in range(8):
            self.dma(b_st[:, :].rearrange("p (blk c) -> p blk c", c=512),
                     d_sout[D * r:D * (r + 1), :].rearrange("(blk p) c -> p blk c",
                                                            p=128))
            self.fl()
            ve.scalar_tensor_tensor(f_S[:, :], b_st[:, :], f_c1[:, r:r + 1], f_S[:, :],
                                    ALU.mult, ALU.add)
            self.bar()
        ve.tensor_copy(b_sbf[:, :], f_S[:, :])
        self.bar()
        self.tap("Sin", b_sbf[:, 0:1024])

        WO, WGB = 0, 16384
        load_w(WO, di["Wo"], 1024, 16)
        load_w(WGB, di["Wg"], 2048, 8)

        for tt in range(NT):
            tcols = slice(TT * tt, TT * (tt + 1))
            self.dmaq(2, f_x[:, :].rearrange("p (ic c) -> p ic c", c=TT),
                      di["xT"][:, tcols].rearrange("(ic p) c -> p ic c", p=128))
            lxh = getattr(self, "_lxh", {})
            self._lxh = lxh
            lxh[tt] = self.dmaq(2, b_h1[:, :].rearrange("p (ic c) -> p ic c", c=TT),
                                d_h1[:, tcols].rearrange("(ic p) c -> p ic c", p=128))
            self.fl()

            def gt_mms(i, bank):
                for ic in range(8):
                    last = te.matmul(
                        bank[:, :],
                        wa[:, WGB + ic * 2048 + 128 * i:WGB + ic * 2048 + 128 * (i + 1)],
                        b_h1[:, 512 * ic:512 * (ic + 1)],
                        start=(ic == 0), stop=(ic == 7))
                return last

            def gt_evac(i, bank):
                return se.activation(b_v[:, 512 * i:512 * (i + 1)], bank[:, :], AF.Silu)
            self.pingpong(16, gt_mms, gt_evac, [PS[0], PS[1]])
            self.tap("sgt", b_v[:, 0:1024])

            corrbank = [PS[3], PS[5], PS[2], PS[6]]
            c1 = {}
            v1 = {}
            a1 = {}
            c2 = {}
            a2 = {}
            c3 = {}
            v2 = {}
            vstt = {}
            lsc = {}
            ofull = [f_c1[:, 0:512], f_c1[:, 512:1024],
                     f_c0[:, 0:512], f_c0[:, 512:1024]]
            qcb = [b_st[:, 0:1024], b_h1[:, 0:1024]]

            def sc_loads(s4):
                if s4 > 3:
                    return
                sc = 4 * tt + s4
                pp = s4 % 2
                for h in range(H):
                    self.dmaq(pp, b_k[:, 2048 * pp + 512 * h:2048 * pp + 512 * (h + 1)],
                              d_oloc[sc, h])
                lsc[s4] = self.dmaq(
                    pp, qcb[pp][:, :].rearrange("p (dc c) -> p dc c", c=128),
                    d_qc[sc].rearrange("(dc p) c -> p dc c", p=128))
            sc_loads(0)
            sc_loads(1)
            for s4 in range(4):
                sc = 4 * tt + s4
                pp = s4 % 2
                for h in range(H):
                    self.w("te", f"q{pp}", lsc[s4])
                    self.w("te", "A", a2.get((s4 - 1, h), 0))
                    self.w("te", "V", v1.get((s4 - 1, h), 0))
                    for m4 in range(4):
                        oc = 128 * m4
                        for dc in range(2):
                            blk = 2 * h + dc
                            last = te.matmul(
                                corrbank[h][:, oc:oc + 128],
                                b_sbf[:, 512 * blk + oc:512 * blk + oc + 128],
                                qcb[pp][:, 128 * blk:128 * (blk + 1)],
                                start=(dc == 0), stop=(dc == 1))
                    c1[(s4, h)] = self.tag("T", last)
                if BP >= 1:
                    for h in range(H):
                        self.w("ve", "T", c1[(s4, h)])
                        self.w("ve", f"q{pp}", lsc[s4])
                        v1[(s4, h)] = self.tag("V", ve.tensor_tensor(
                            ofull[h], corrbank[h][:, :],
                            b_k[:, 2048 * pp + 512 * h:2048 * pp + 512 * (h + 1)],
                            ALU.add))
                self.w("sy", "T", c1[(s4, 3)])
                if BP >= 1:
                    self.w("sy", "V", v1[(s4, 3)])
                sc_loads(s4 + 2)
                if BP < 2:
                    continue
                for h in range(H):
                    self.w("se", "V", v1[(s4, h)])
                    a1[(s4, h)] = self.tag("A", se.activation(
                        b_xb[:, 512 * h:512 * (h + 1)], ofull[h], AF.Square))
                for h in range(H):
                    self.w("te", "A", a1[(s4, h)])
                    for m4 in range(4):
                        last = te.matmul(corrbank[h][0:1, 0:128], ones_col,
                                         b_xb[:, 512 * h + 128 * m4:
                                              512 * h + 128 * (m4 + 1)],
                                         start=(m4 == 0), stop=(m4 == 3))
                    c2[(s4, h)] = self.tag("T", last)
                for h in range(H):
                    row = f_c1[0:1, 1024 + 128 * (h % 2):1152 + 128 * (h % 2)]
                    self.w("se", "T", c2[(s4, h)])
                    self.ack(se.activation(row, corrbank[h][0:1, 0:128], AF.Ln,
                                           scale=cv1(1.0 / 512.0), bias=cv1(1e-5)))
                    a2[(s4, h)] = self.tag("A", se.activation(
                        b_e[0:1, 128 * h:128 * (h + 1)], row, AF.Exp,
                        scale=cv1(-0.5)))
                if BP < 3:
                    continue
                for h in range(H):
                    rrb = PS[1] if h % 2 == 0 else PS[4]
                    self.w("te", "A", a2[(s4, h)])
                    self.w("te", "V", v2.get((s4, h - 2), 0) if h >= 2 else
                           v2.get((s4 - 1, 2 + h), 0))
                    for m4 in range(4):
                        last = te.matmul(rrb[:, 128 * m4:128 * (m4 + 1)],
                                         b_rw[:, 128 * m4:128 * (m4 + 1)],
                                         b_e[0:1, 128 * h:128 * (h + 1)],
                                         start=True, stop=True)
                    c3[(s4, h)] = self.tag("T", last)
                    self.w("ve", "T", c3[(s4, h)])
                    self.w("ve", "V", v1[(s4, h)])
                    self.vck(ve.tensor_tensor(ofull[h], ofull[h], rrb[:, :],
                                              ALU.mult))
                    v2[(s4, h)] = self.tag("V", ve.tensor_tensor(
                        b_st[:, 2048 + 512 * h:2048 + 512 * (h + 1)].rearrange(
                            "p (a b) -> p a b", b=128),
                        ofull[h].rearrange("p (a b) -> p a b", b=128),
                        bass.AP(b_v, (4 * h) * 512 + 128 * s4,
                                [[8192, 128], [512, 4], [1, 128]]), ALU.mult))
                if BP < 4:
                    continue
                btag = {}
                for jc in range(8):
                    psb = PS[7] if jc < 4 else PS[0]
                    oc = 128 * (jc % 4)
                    self.w("te", "V", v2[(s4, 3)])
                    self.w("te", "V", vstt.get((s4 - 1, 3 if jc < 4 else 7), 0))
                    first = True
                    for h in range(H):
                        for m4 in range(4):
                            ec = 4 * h + m4
                            last = te.matmul(
                                psb[:, oc:oc + 128],
                                wa[:, WO + ec * 1024 + 128 * jc:
                                   WO + ec * 1024 + 128 * (jc + 1)],
                                b_st[:, 2048 + 512 * h + 128 * m4:
                                     2048 + 512 * h + 128 * (m4 + 1)],
                                start=first, stop=(ec == 15))
                            first = False
                    if jc in (3, 7):
                        btag[jc] = self.tag("T", last)
                for jc in range(8):
                    psb = PS[7] if jc < 4 else PS[0]
                    oc = 128 * (jc % 4)
                    self.w("ve", "T", btag[3 if jc < 4 else 7])
                    self.w("ve", "q2", lxh[tt])
                    vstt[(s4, jc)] = self.tag("V", ve.scalar_tensor_tensor(
                        f_P[:, jc * 512 + 128 * s4:jc * 512 + 128 * (s4 + 1)],
                        psb[:, oc:oc + 128], f_m[:, M_GA + jc:M_GA + jc + 1],
                        f_x[:, jc * 512 + 128 * s4:jc * 512 + 128 * (s4 + 1)],
                        ALU.mult, ALU.add))
            self.bar()
            self.dma(d_xmid[:, tcols].rearrange("(ic p) c -> p ic c", p=128),
                     f_P[:, :].rearrange("p (ic c) -> p ic c", c=TT))
            self.tap("xmid", f_P[:, 0:1024])
            ve.tensor_copy(b_xb[:, :], f_P[:, :])
            self.bar()
            ln_block(b_xb, b_h1, M_SB, M_CB1, b_h1)
            self.tap("h2", b_h1[:, 0:1024])
            for ic in range(8):
                ve.tensor_reduce(f_c1[:, 1024 + ic:1024 + ic + 1],
                                 b_h1[:, 512 * ic:512 * (ic + 1)], AX.X, ALU.max,
                                 apply_absolute_value=True)
            self.bar()
            ve.tensor_reduce(f_c1[:, 1032:1033], f_c1[:, 1024:1032], AX.X, ALU.max)
            self.bar()
            ve.tensor_tensor(f_m[:, M_RM0:M_RM0 + 1], f_m[:, M_RM0:M_RM0 + 1],
                             f_c1[:, 1032:1033], ALU.max)
            self.bar()
            self.dma(d_h2[:, tcols].rearrange("(ic p) c -> p ic c", p=128),
                     b_h1[:, :].rearrange("p (ic c) -> p ic c", c=TT))
            self.fl()
        if PHASES < 5:
            return self.finish()

        def fold_max(cols, ar_k):
            arin, arout = ar_io[ar_k]
            for idx, c in enumerate(cols):
                self.dma(d_pmax[:, :], f_m[:, c:c + 1])
                self.fl()
                self.dma(f_c1[0:1, 0:128], bass.AP(d_pmax, 0, [[1, 1], [1, 128]]))
                self.fl()
                ve.tensor_reduce(f_c1[0:1, 128 + idx:129 + idx], f_c1[0:1, 0:128],
                                 AX.X, ALU.max)
                self.bar()
            self.dma(f_c1[0:1, 256:264], di["oneh"][:, :])
            self.fl()
            for idx in range(len(cols)):
                ve.tensor_scalar(f_c1[0:1, 192 + 8 * idx:200 + 8 * idx],
                                 f_c1[0:1, 256:264], f_c1[0:1, 128 + idx:129 + idx],
                                 None, ALU.mult)
            if len(cols) == 1:
                ge.memset(f_c1[0:1, 200:208], 0.0)
            self.bar()
            self.dma(arin[:, :], f_c1[0:1, 192:208])
            self.fl()
            ge.collective_compute("AllReduce", ALU.add,
                                  replica_groups=[list(range(8))],
                                  ins=[arin[:, :]], outs=[arout[:, :]]).then_inc(ccsem)
            cc[0] += 1
            ge.wait_ge(ccsem, cc[0])
            self.bar()
            self.dma(f_c1[0:1, 0:16], arout[:, :])
            self.fl()
            for idx in range(len(cols)):
                ve.tensor_reduce(f_c1[0:1, 16 + idx:17 + idx],
                                 f_c1[0:1, 8 * idx:8 * idx + 8], AX.X, ALU.max)
            self.bar()
            return [f_c1[0:1, 16 + idx:17 + idx] for idx in range(len(cols))]

        def qscale(maxap, scol, slot):
            ve.tensor_scalar(f_c1[0:1, 32:33], maxap, 1.0 / QMAX, 1e-12, ALU.mult,
                             ALU.max)
            self.bar()
            ve.reciprocal(f_c1[0:1, 33:34], f_c1[0:1, 32:33])
            self.bar()
            self.dma(d_scal[0:1, slot:slot + 2], f_c1[0:1, 32:34])
            self.fl()
            self.dma(f_m[:, scol:scol + 1],
                     bass.AP(d_scal, slot, [[0, 128], [1, 1]]))
            self.dma(f_m[:, scol + 1:scol + 2],
                     bass.AP(d_scal, slot + 1, [[0, 128], [1, 1]]))
            self.fl()

        def bcast_scalar(srcap, dstcol, slot):
            self.dma(d_scal[0:1, slot:slot + 1], srcap)
            self.fl()
            self.dma(f_m[:, dstcol:dstcol + 1],
                     bass.AP(d_scal, slot, [[0, 128], [1, 1]]))
            self.fl()

        mx = fold_max([M_RM0], 0)[0]
        qscale(mx, M_SCAL, 0)
        S0, R0 = M_SCAL, M_SCAL + 1
        self.tap("sc0", f_m[:, 72:80])

        fxi = f_x.bitcast(I16)
        fxu = f_x.bitcast(U16)
        bsti = b_st.bitcast(I16)
        self.bar()

        def dequant(src, dst, n_oc, icols):
            nb = icols // 32
            tiles = [f_c0[:, 0:icols], (f_c1 if icols == 1024 else f_P)[:, 0:icols]]
            ls, vtile, vout, gt_ = {}, {}, {}, {}
            so_ = [1024, 1152] if icols == 1024 else [0, 192]
            dso = 64 if icols == 1024 else 96
            kis = [i_carve[:, 0:icols], fxi[:, 0:icols]]
            stg_ = [b_st[:, 0:icols], b_xb[:, 0:icols]]
            ls[0] = self.dmaq(0, tiles[0], src[0:128, :])
            if n_oc > 1:
                ls[1] = self.dmaq(1, tiles[1], src[128:256, :])

            def emit(step, oc):
                if oc >= n_oc:
                    return
                p = oc % 2
                tile = tiles[p]
                so = so_[p]
                sv = f_c1[:, so:so + nb]
                rv = f_c1[:, so + dso:so + dso + nb]
                if step == 0:
                    self.w("ve", f"q{p}", ls[oc])
                    self.tag("V", ve.tensor_reduce(
                        sv.rearrange("p (a b) -> p a b", b=1),
                        tile.rearrange("p (a b) -> p a b", b=32), AX.X,
                        ALU.max, apply_absolute_value=True))
                elif step == 1:
                    self.w("ve", "V", self.mc["V"])
                    self.tag("V", ve.tensor_scalar(sv, sv, 1.0 / 7.0, 1e-12,
                                                   ALU.mult, ALU.max))
                elif step == 2:
                    self.w("ve", "V", self.mc["V"])
                    self.tag("V", ve.reciprocal(rv, sv))
                elif step == 3:
                    self.w("ve", "V", self.mc["V"])
                    vtile[oc] = self.tag("V", ve.tensor_tensor(
                        kis[p].rearrange("p (a b) -> p a b", b=32),
                        tile.rearrange("p (a b) -> p a b", b=32),
                        bass.AP(f_c1, so + dso, [[1280, 128], [1, nb], [0, 32]]),
                        ALU.mult))
                    if oc + 2 < n_oc:
                        self.w("sy", "V", vtile[oc])
                        ls[oc + 2] = self.dmaq(p, tiles[p],
                                               src[128 * (oc + 2):128 * (oc + 3), :])
                else:
                    self.w("ve", "V", self.mc["V"])
                    self.w("ve", f"q{2 + p}", gt_.get(oc - 2, 0))
                    vout[oc] = self.tag("V", ve.tensor_tensor(
                        stg_[p].rearrange("p (a b) -> p a b", b=32),
                        kis[p].rearrange("p (a b) -> p a b", b=32),
                        bass.AP(f_c1, so, [[1280, 128], [1, nb], [0, 32]]),
                        ALU.mult))
                    self.w("sy", "V", vout[oc])
                    gt_[oc] = self.dmaq(2 + p, dst[128 * oc:128 * (oc + 1), :],
                                        stg_[p], queue="ge")
            for oc0 in range(0, n_oc, 2):
                for step in range(5):
                    emit(step, oc0)
                    emit(step, oc0 + 1)
            self.fl()
        dequant(di["wup"], d_wdq[0], 24, 1024)
        self.tap("dwq", d_wdq[0][0:128, :])
        dequant(di["wgate"], d_wdq[1], 24, 1024)
        dequant(di["wdown"], d_wdqd, 8, 3072)
        self.tap("wdq", b_st[:, 0:1024])

        ge.memset(f_m[:, M_RM0:M_RM0 + 2], 0.0)
        self.bar()
        for gi, (d_raw, runc) in enumerate([(d_up, M_RM0), (d_gate, M_RM1)]):
            for ic in range(8):
                self.dma(wa[:, 3072 * ic:3072 * (ic + 1)],
                         d_wdq[gi][:, 128 * ic:128 * (ic + 1)], transpose=True)
            self.fl()
            if gi == 0:
                self.tap("wat", wa[:, 0:1024])
            lt = {}
            vq = {}
            ttag = {}
            atag = {}
            vred = {}
            gtag = {}
            ptag = {}
            h2bufs = [b_k, b_q]
            h2qbufs = [b_xb, b_h1]
            for tt in range(NT):
                tcols = slice(TT * tt, TT * (tt + 1))
                p = tt % 2
                self.w("sy", "V", ptag.get(tt - 2, 0))
                lt[tt] = self.dmaq(
                    p, h2bufs[p][:, :].rearrange("p (ic c) -> p ic c", c=TT),
                    d_h2[:, tcols].rearrange("(ic p) c -> p ic c", p=128))
                k0 = i_carve[:, 4096 * p:4096 * (p + 1)]
                self.w("ve", f"q{p}", lt[tt])
                ptag[tt] = self.vck(ve.tensor_scalar(
                    k0, h2bufs[p][:, :], f_m[:, R0:R0 + 1], None, ALU.mult))
                self.mc["P"] += 0
                self.w("ve", "T", ttag.get((tt - 2, 23), 0))
                vq[tt] = self.tag("V", ve.tensor_scalar(
                    h2qbufs[p][:, :], k0, f_m[:, S0:S0 + 1], None, ALU.mult))
                for i in range(24):
                    self.w("te", "V", vq[tt])
                    g4 = i // 4
                    self.w("te", "A", atag.get((tt, i - 2), 0) if i >= 2 else
                           atag.get((tt - 1, 22 + i), 0))
                    bank = PS[i % 2]
                    for ic in range(8):
                        last = te.matmul(
                            bank[:, :],
                            wa[:, 3072 * ic + 128 * i:3072 * ic + 128 * (i + 1)],
                            h2qbufs[p][:, 512 * ic:512 * (ic + 1)],
                            start=(ic == 0), stop=(ic == 7))
                    ttag[(tt, i)] = self.tag("T", last)
                    stg = b_st[:, 2048 * (g4 % 2) + 512 * (i % 4):
                               2048 * (g4 % 2) + 512 * (i % 4) + 512]
                    self.w("se", "T", ttag[(tt, i)])
                    self.w("se", f"q{2 + g4 % 2}",
                           gtag.get((tt, g4 - 2), 0) if g4 >= 2 else
                           gtag.get((tt - 1, 4 + g4), 0))
                    self.w("se", "V", vred.get((tt, 4 * (g4 - 2) + 3), 0) if g4 >= 2
                           else vred.get((tt - 1, 4 * (4 + g4) + 3), 0))
                    atag[(tt, i)] = self.tag("A", se.copy(stg, bank[:, :]))
                    self.w("ve", "A", atag[(tt, i)])
                    vred[(tt, i)] = self.tag("V", ve.tensor_reduce(
                        f_c1[:, 32 + i:33 + i], stg, AX.X, ALU.max,
                        apply_absolute_value=True))
                    if gi == 0 and tt == 1 and i == 3:
                        self.tap("stg1", b_st[:, 0:2048])
                    if i % 4 == 3:
                        oc0 = g4 * 4
                        self.w("sy", "A", atag[(tt, i)])
                        for o4 in range(4):
                            gtag[(tt, g4)] = self.dmaq(
                                2 + g4 % 2,
                                d_raw[oc0 + o4][:, tcols],
                                b_st[:, 2048 * (g4 % 2) + 512 * o4:
                                     2048 * (g4 % 2) + 512 * (o4 + 1)], queue="ge")
                self.w("ve", "V", vred[(tt, 23)])
                self.vck(ve.tensor_reduce(f_c1[:, 0:1], f_c1[:, 32:56], AX.X,
                                          ALU.max))
                self.vck(ve.tensor_tensor(f_m[:, runc:runc + 1],
                                          f_m[:, runc:runc + 1], f_c1[:, 0:1],
                                          ALU.max))
            self.fl()
        self.tap("dup", d_up[0][:, 0:1024])
        self.tap("dgt", d_gate[0][:, 0:1024])
        self.tap("dup7", d_up[23][:, T - 1024:T])
        if PHASES < 6:
            return self.finish()

        mup, mgt = fold_max([M_RM0, M_RM1], 1)
        qscale(mup, M_SCAL + 2, 2)
        qscale(mgt, M_SCAL + 4, 4)
        S1U, R1U, S1G, R1G = M_SCAL + 2, M_SCAL + 3, M_SCAL + 4, M_SCAL + 5
        self.tap("sc1", f_m[:, 72:80])

        ge.memset(f_m[:, M_RM0:M_RM0 + 1], 0.0)
        ve.tensor_scalar(f_c1[0:1, 34:35], f_m[0:1, S1G:S1G + 1], 1.0 / 65536.0, None,
                         ALU.mult)
        self.bar()
        bcast_scalar(f_c1[0:1, 34:35], 80, 6)
        C1G = 80
        ls = {}
        ptagk = {}
        ptagy = {}
        vt2 = {}
        gtag = {}
        inbufs = [b_k, b_q]
        sbufs = [f_c0, f_P]
        stages = [b_v[:, 0:4096], b_v[:, 4096:8192], b_st[:, 0:4096],
                  b_xb[:, 0:4096]]

        def p2_spill(oc):
            if oc < 0:
                return
            self.w("sy", "V", vt2[oc])
            gtag[oc] = self.dmaq(2 + oc % 2, d_t2[oc][:, :], stages[oc % 4],
                                 queue="ge")
        for oc in range(24):
            p = oc % 2
            self.w("sy", "V", ptagk.get(oc - 2, 0))
            ls[oc] = self.dmaq(p, inbufs[p][:, :], d_gate[oc][:, :])
            p2_spill(oc - 2)
            kg = i_carve[:, 4096 * p:4096 * (p + 1)]
            yu = fxu[:, 4096 * p:4096 * (p + 1)]
            self.w("ve", f"q{p}", ls[oc])
            ptagk[oc] = self.vck(ve.tensor_scalar(
                kg, inbufs[p][:, :], f_m[:, R1G:R1G + 1], None, ALU.mult))
            self.w("se", "V", ptagk[oc])
            self.w("se", "A", ptagy.get(oc - 2, 0))
            self.ack(se.activation(
                sbufs[p][:, :], kg, AF.Sigmoid, scale=f_m[:, S1G:S1G + 1]))
            ptagy[oc] = self.tag("A", se.activation(
                yu, sbufs[p][:, :], AF.Identity, scale=cv(65536.0)))
            stage = stages[oc % 4]
            self.w("ve", "A", ptagy[oc])
            self.w("ve", f"q{2 + oc % 2}", gtag.get(oc - 4, 0))
            self.vck(ve.scalar_tensor_tensor(stage, kg, f_m[:, C1G:C1G + 1], yu,
                                             ALU.mult, ALU.mult))
            vt2[oc] = self.tag("V", ve.tensor_reduce(
                f_c1[:, 32 + oc:33 + oc], stage, AX.X, ALU.max,
                apply_absolute_value=True))
        p2_spill(22)
        p2_spill(23)
        self.w("ve", "V", vt2[23])
        self.vck(ve.tensor_reduce(f_m[:, M_RM0:M_RM0 + 1], f_c1[:, 32:56], AX.X,
                                  ALU.max))
        self.fl()
        self.tap("t2", b_v[:, 0:1024])
        if PHASES < 7:
            return self.finish()

        mt2 = fold_max([M_RM0], 2)[0]
        qscale(mt2, 81, 7)
        S2, R2 = 81, 82
        ve.tensor_tensor(f_c1[0:1, 34:35], f_m[0:1, S2:S2 + 1],
                         f_m[0:1, S1U:S1U + 1], ALU.mult)
        self.bar()
        bcast_scalar(f_c1[0:1, 34:35], 83, 9)
        C2 = 83

        ge.memset(f_m[:, M_RM0:M_RM0 + 1], 0.0)
        self.bar()
        for oc in range(24):
            self.dma(wa[:, 1024 * oc:1024 * (oc + 1)],
                     d_wdqd[:, 128 * oc:128 * (oc + 1)], transpose=True)
        ls = {}
        ls2 = {}
        ptag = {}
        vop = {}
        gtag = {}
        t2bufs = [b_k, b_h1]
        upbufs = [b_q, b_xb]
        fpb = f_P.bitcast(BF16)
        stages = [b_v[:, 0:4096], b_v[:, 4096:8192], b_st[:, 0:4096],
                  fpb[:, 0:4096]]

        def p3_spill(oc):
            if oc < 0:
                return
            self.w("sy", "V", vop[oc])
            gtag[oc] = self.dmaq(2 + oc % 2, d_opre[oc][:, :], stages[oc % 4],
                                 queue="ge")
        for oc in range(24):
            p = oc % 2
            self.w("sy", "A", ptag.get(oc - 2, 0))
            ls[oc] = self.dmaq(p, t2bufs[p][:, :], d_t2[oc][:, :])
            ls2[oc] = self.dmaq(4 + p, upbufs[p][:, :], d_up[oc][:, :])
            p3_spill(oc - 2)
            k3 = i_carve[:, 4096 * p:4096 * (p + 1)]
            k2 = fxi[:, 4096 * p:4096 * (p + 1)]
            self.w("se", f"q{p}", ls[oc])
            self.w("se", f"q{4 + p}", ls2[oc])
            se.activation(k3, t2bufs[p][:, :], AF.Identity, scale=f_m[:, R2:R2 + 1])
            ptag[oc] = self.tag("A", se.activation(
                k2, upbufs[p][:, :], AF.Identity, scale=f_m[:, R1U:R1U + 1]))
            stage = stages[oc % 4]
            self.w("ve", "A", ptag[oc])
            self.w("ve", f"q{2 + oc % 2}", gtag.get(oc - 4, 0))
            self.vck(ve.scalar_tensor_tensor(stage, k3, f_m[:, C2:C2 + 1], k2,
                                             ALU.mult, ALU.mult))
            vop[oc] = self.tag("V", ve.tensor_reduce(
                f_c1[:, 32 + oc:33 + oc], stage, AX.X, ALU.max,
                apply_absolute_value=True))
        p3_spill(22)
        p3_spill(23)
        self.w("ve", "V", vop[23])
        self.vck(ve.tensor_reduce(f_m[:, M_RM0:M_RM0 + 1], f_c1[:, 32:56], AX.X,
                                  ALU.max))
        self.fl()

        if PHASES < 8:
            return self.finish()

        mop = fold_max([M_RM0], 3)[0]
        qscale(mop, 84, 10)
        S3, R3 = 84, 85

        self.fl()
        oqbufs = [b_k, b_q, b_h1]
        oqcarv = [i_carve[:, 0:4096], i_carve[:, 4096:8192],
                  bsti[:, 0:4096]]

        def oq_ap(oc):
            return oqbufs[oc // 8][:, 512 * (oc % 8):512 * (oc % 8) + 512]
        lx = {}
        lo = {}
        ptag = {}
        voq = {}
        ttag = {}
        vout = {}
        stout = {}
        for tt in range(NT):
            tcols = slice(TT * tt, TT * (tt + 1))
            self.w("sy", "T", ttag.get((tt - 1, 7), 0))
            self.w("sy", "V", vout.get((tt - 1, 7), 0))
            pq = tt % 2
            lx[tt] = self.dmaq(pq, f_x[:, :].rearrange("p (ic c) -> p ic c", c=TT),
                               d_xmid[:, tcols].rearrange("(ic p) c -> p ic c", p=128))
            for g3 in range(3):
                lo[(tt, g3)] = self.dmaq(
                    pq, oqbufs[g3][:, :].rearrange("p (o c) -> p o c", c=512),
                    d_opre[8 * g3:8 * (g3 + 1)][:, :, tcols].rearrange(
                        "o p c -> p o c"))
            for g3 in range(3):
                self.w("se", f"q{pq}", lo[(tt, 2)])
                ptag[(tt, g3)] = self.tag("A", se.activation(
                    oqcarv[g3], oqbufs[g3][:, :], AF.Identity,
                    scale=f_m[:, R3:R3 + 1]))
                self.w("ve", "A", ptag[(tt, g3)])
                voq[(tt, g3)] = self.tag("V", ve.tensor_scalar(
                    oqbufs[g3][:, :], oqcarv[g3], f_m[:, S3:S3 + 1], None, ALU.mult))
            for jc in range(8):
                bank = PS[jc % 2]
                for g3 in range(3):
                    self.w("te", "V", voq[(tt, g3)])
                self.w("te", "V", vout.get((tt, jc - 2), 0) if jc >= 2 else
                       vout.get((tt - 1, 6 + jc), 0))
                for oc in range(24):
                    last = te.matmul(
                        bank[:, :],
                        wa[:, 1024 * oc + 128 * jc:1024 * oc + 128 * (jc + 1)],
                        oq_ap(oc), start=(oc == 0), stop=(oc == 23))
                ttag[(tt, jc)] = self.tag("T", last)
                stg = f_c0[:, 512 * (jc % 2):512 * (jc % 2) + 512]
                self.w("ve", "T", ttag[(tt, jc)])
                self.w("ve", f"q{pq}", lo[(tt, 2)])
                self.w("ve", f"q{6 + jc % 2}",
                       stout.get((tt, jc - 2), 0) if jc >= 2 else
                       stout.get((tt - 1, 6 + jc), 0))
                vout[(tt, jc)] = self.tag("V", ve.scalar_tensor_tensor(
                    stg, bank[:, :], f_m[:, M_GB + jc:M_GB + jc + 1],
                    f_x[:, 512 * jc:512 * (jc + 1)], ALU.mult, ALU.add))
                self.w("sy", "V", vout[(tt, jc)])
                stout[(tt, jc)] = self.dmaq(6 + jc % 2,
                                            outT[128 * jc:128 * (jc + 1), tcols], stg)
        return self.finish()

    def finish(self):
        self.fl()
        return self.nc


def build(taps=()):
    b = Bld(taps)
    with b.ctx:
        nc = b.build()
    return nc


def kernel(**inputs):
    x = np.asarray(inputs["L0_attn_hidden_states"], np.float32)
    c = np.asarray(inputs["L0_c"], np.float32)
    bgk2 = np.asarray(inputs["L0_attn_bgk2"], np.float32)
    rmsw = np.asarray(inputs["L0_attn_rms_w"], np.float32)

    base = {
        "Wq": np.ascontiguousarray(inputs["L0_attn_Wq"], np.float32),
        "Wk": np.ascontiguousarray(inputs["L0_attn_Wk"], np.float32),
        "Wv": np.ascontiguousarray(inputs["L0_attn_Wv"], np.float32),
        "Wg": np.ascontiguousarray(inputs["L0_attn_Wg"], np.float32),
        "Wgk1": np.ascontiguousarray(inputs["L0_attn_Wgk1"], np.float32),
        "Wgk2": np.ascontiguousarray(inputs["L0_attn_Wgk2"], np.float32),
        "Wo": np.ascontiguousarray(inputs["L0_attn_Wo"], np.float32),
        "wup": np.ascontiguousarray(inputs["L0_ffn_w_up"], np.float32),
        "wgate": np.ascontiguousarray(inputs["L0_ffn_w_gate"], np.float32),
        "wdown": np.ascontiguousarray(inputs["L0_ffn_w_down"], np.float32),
        "wmod": np.ascontiguousarray(inputs["L0_wmod"], np.float32),
        "bgk2f": np.ascontiguousarray(bgk2.reshape(8, 128).T),
        "rmsw": np.ascontiguousarray(rmsw.reshape(1, 512)),
    }
    nc = build(DEBUG_TAPS)
    in_maps = []
    for core in range(8):
        b_, s_ = divmod(core, 2)
        m = dict(base)
        m["xT"] = np.ascontiguousarray(x[b_, 4096 * s_:4096 * (s_ + 1), :].T)
        m["cfm"] = np.ascontiguousarray(c[b_].reshape(8, 128).T)
        sel = np.zeros((1, 8), np.float32)
        if s_ == 1:
            sel[0, core - 1] = 1.0
        m["selS"] = sel
        oh = np.zeros((1, 8), np.float32)
        oh[0, core] = 1.0
        m["oneh"] = oh
        in_maps.append(m)
    res = run_bass_kernel_spmd(nc, in_maps, core_ids=list(range(8)))
    out = np.empty((4, 8192, 1024), np.float32)
    for core in range(8):
        b_, s_ = divmod(core, 2)
        out[b_, 4096 * s_:4096 * (s_ + 1), :] = res.results[core]["outT"].T
    kernel._results = res.results
    return out
